# version 08
# baseline (speedup 1.0000x reference)
import contextlib
import numpy as np
import ml_dtypes
import concourse.bass as bass
import concourse.mybir as mybir
from concourse.bass_utils import run_bass_kernel_spmd

F32 = mybir.dt.float32
BF16 = mybir.dt.bfloat16
AF = mybir.ActivationFunctionType
ALU = mybir.AluOpType
AX = mybir.AxisListType
BF = ml_dtypes.bfloat16

B, T, D, H, CF, DH = 2, 8192, 1024, 16, 4, 64
TL, NT, NSEG = 2048, 16, 4
M, NMC = 2176, 17
FF, FS, NE, NS = 4096, 512, 8, 8

ENGINES = ("tensor", "vector", "scalar", "sync", "gpsimd")
DEBUG_TAPS = []
PHASE_LIM = 99
MASK_OFF = [0, 14, 29, 45]


class Sched:

    def __init__(self):
        self.ops = {e: [] for e in ENGINES}
        self.cnt = {}

    def val(self, sem):
        return self.cnt.get(sem, 0)

    def emit(self, eng, fn, waits=(), inc=None):
        if inc is not None:
            self.cnt[inc[0]] = self.cnt.get(inc[0], 0) + inc[1]
        self.ops[eng].append((tuple(w for w in waits if w and w[1] > 0), inc, fn))
        return self.cnt[inc[0]] if inc is not None else None

    def dma_pcf(self, q, dst, src, c, p=128, waits=(), sem="d0"):
        return self.dma(q, dst.rearrange("q (c f) -> q c f", c=c),
                        src.rearrange("(c q) f -> q c f", q=p),
                        waits=waits, sem=sem)

    def dma(self, q, out, in_, waits=(), slow=False, sem="d0"):
        def fn(e, out=out, in_=in_, slow=slow):
            if slow:
                return e.dma_start(out=out, in_=in_, allow_slow_non_contiguous=True)
            return e.dma_start(out=out, in_=in_)
        return self.emit(q, fn, waits=waits, inc=(sem, 16))

    def terminal(self):
        waits = [(k, v) for k, v in self.cnt.items()]
        self.ops["sync"].append((tuple(w for w in waits if w[1] > 0), None,
                                 lambda e: e.nop()))

    def replay(self, block, sems):
        for eng_name in ENGINES:
            oplist = self.ops[eng_name]

            def body(e, oplist=oplist):
                for waits, inc, fn in oplist:
                    for (s, v) in waits:
                        e.wait_ge(sems[s], v)
                    ins = fn(e)
                    if inc is not None:
                        ins.then_inc(sems[inc[0]], inc[1])
            getattr(block, eng_name)(body)


def _constants():
    c = {}
    c["ident"] = np.eye(128, dtype=np.float32)
    d_ = np.arange(D)
    ang_d = 2.0 * np.pi * np.outer(d_, d_) / D
    c["cd_in"] = np.cos(ang_d).astype(BF)
    c["sd_in"] = np.sin(ang_d).astype(BF)
    beta = np.arange(128)
    kap = np.arange(128)
    GRe = np.empty((64, 128, 128), np.float32)
    GIm = np.empty((64, 128, 128), np.float32)
    for a in range(64):
        ang = 2.0 * np.pi * (np.outer(beta, kap) / 128.0 + a * kap[None, :] / 8192.0)
        GRe[a] = np.cos(ang)
        GIm[a] = -np.sin(ang)
    c["gre_in"] = GRe.reshape(64 * 128, 128).astype(BF)
    c["gim_in"] = GIm.reshape(64 * 128, 128).astype(BF)
    c["gren_in"] = (-GRe).reshape(64 * 128, 128).astype(BF)
    return c


def _core_masks(s):
    dm = np.arange(128)[:, None]
    dt = np.arange(512)[None, :]
    D0 = (dt >= 4 * dm - 1).astype(np.float32)
    D1 = ((dm == 0) & (dt == 511)).astype(np.float32)
    ONES = np.ones((128, 512), np.float32)
    ZERO = np.zeros((128, 512), np.float32)
    tiles = []
    for p in range(NSEG):
        c_t = 4 * s + p
        for cm in range(14 + p):
            tiles.append(ONES if cm < c_t else
                         D0 if cm == c_t else
                         D1 if cm == c_t + 1 else ZERO)
    return np.concatenate(tiles, axis=0).astype(BF)


def _core_f64(s):
    lam = 16 * s + np.arange(16)
    alpha = np.arange(64)
    ang = 2.0 * np.pi * np.outer(alpha, lam) / 64.0
    return np.cos(ang).astype(BF), np.sin(ang).astype(BF)


def build_nc():
    nc = bass.Bass("TRN2", target_bir_lowering=False)
    dram = {}

    def din(name, shape, dtype=F32):
        dram[name] = nc.dram_tensor(name, list(shape), dtype, kind="ExternalInput")

    for name, shape, dt in [
        ("x_sl", (TL, D), F32), ("wqkv", (D, 3 * D), F32),
        ("conv_tw", (16, 4, 64, 64), F32), ("conv_b", (D,), F32),
        ("conv_bb", (128, D), F32), ("null_k", (D,), F32), ("null_v", (D,), F32),
        ("w_out", (D, D), F32), ("b_out", (D,), F32), ("gate_wt", (D, NE), F32),
        ("w1_e", (NE, D, FF), F32), ("w3_e", (NE, D, FF), F32),
        ("w2_e", (NE, FF, D), F32), ("w1_s", (NS, D, FS), F32),
        ("w3_s", (NS, D, FS), F32), ("w2_s", (NS, FS, D), F32),
        ("ident", (128, 128), F32), ("cd_in", (D, D), BF16),
        ("sd_in", (D, D), BF16), ("gre_in", (64 * 128, 128), BF16),
        ("gim_in", (64 * 128, 128), BF16), ("gren_in", (64 * 128, 128), BF16),
        ("f64c_in", (64, 16), BF16), ("f64s_in", (64, 16), BF16),
        ("masks_in", (62 * 128, 512), BF16),
    ]:
        din(name, shape, dt)

    dram["y_out"] = nc.dram_tensor("y_out", [TL, D], F32, kind="ExternalOutput")
    taps = {}
    tap_shapes = {
        "xnT": ((128, 8 * TL), BF16), "gates": ((128, NT * NE), F32),
        "ffn": ((128, NT * D), BF16), "outT": ((128, 8 * TL), BF16),
        "yT": ((128, 8 * TL), BF16), "kcT": ((64, 16 * 512), BF16),
        "vcL": ((128, 4 * 1024), BF16), "ab": ((4096, 1024), BF16),
        "xscr": ((TL, D), F32),
    }
    for t in DEBUG_TAPS:
        shp, dt = tap_shapes[t]
        taps[t] = nc.dram_tensor("tap_" + t, list(shp), dt, kind="ExternalOutput")

    for n, shape, dt in [
        ("qt_dram", (D, TL), BF16), ("cc1_in", (1032, 1024), BF16),
        ("cc1_out", (4 * 1032, 1024), BF16), ("cc2_in", (4096, 1024), BF16),
        ("cc2_out", (4 * 4096, 1024), BF16), ("y_dram", (2 * 64 * 128, 1024), BF16),
        ("x_scr", (TL, D), F32),
    ]:
        dram[n] = nc.dram_tensor(n, list(shape), dt)

    S = Sched()
    stack = contextlib.ExitStack()

    def sbuf(name, p, f, dt=BF16):
        return stack.enter_context(nc.sbuf_tensor(name, [p, f], dt))

    sb = {
        "arA": sbuf("arA", 128, 35088),
        "arB": sbuf("arB", 128, 16384),
        "arC": sbuf("arC", 128, 16384),
        "arD": sbuf("arD", 128, 20480),
        "arE": sbuf("arE", 128, 6144, F32),
        "cst": sbuf("cst", 128, 256, F32),
    }

    with (
        nc.psum_tensor("P_A", [128, 2048], F32) as P_A,
        nc.psum_tensor("P_B", [128, 2048], F32) as P_B,
    ):
        sb["P_A"], sb["P_B"] = P_A, P_B
        _emit(S, dram, taps, sb)
        names = sorted(set(S.cnt.keys())
                       | {w[0] for ops in S.ops.values() for (ws, _, _) in ops
                          for w in ws})
        sems = {n: stack.enter_context(nc.semaphore(n)) for n in names}
        with nc.Block() as block:
            S.replay(block, sems)

    stack.close()
    return nc


def _emit(S, dram, taps, sb):
    P_A, P_B = sb["P_A"], sb["P_B"]
    arA, arB, arC, arD, arE, cst = (sb[k] for k in
                                    ("arA", "arB", "arC", "arD", "arE", "cst"))
    dg = lambda n: dram[n].ap()

    xnT = arB
    outT = arB
    ffn = arC
    chunkb = [arD[:, i * 2048:(i + 1) * 2048] for i in range(3)]
    vc_l = arD[:, 6144:6144 + 4160]
    vc_l4 = vc_l.rearrange("p (jc h w) -> p jc h w", jc=4, h=H)
    wq_b = [arD[:, 14336 + i * 1024:14336 + (i + 1) * 1024] for i in range(2)]
    cbb_bf = arD[:, 16384:17408]
    ident_bf = arD[:, 17408:17536]
    gwt_bf = arD[:, 17536:17600]
    xns = [arD[:, 17664 + i * 1024:17664 + (i + 1) * 1024] for i in range(2)]
    convw_sb = arA[:, 0:4096]
    kcT_l = arA[0:64, 4096:4096 + 8192]
    w13_b = [(arD[:, i * 2048:i * 2048 + 1024], arD[:, i * 2048 + 1024:(i + 1) * 2048])
             for i in range(2)]
    w2half = arD[:, 4096:4096 + 16384]
    gu = arA[:, 0:32768]
    kcF = arA[:, 0:8 * M]
    vcF = arA[:, 8 * M:8 * M + NMC * 1040]
    vcF4 = vcF.rearrange("p (c h w) -> p c h w", c=NMC, h=H)
    qTs = arD[:, 0:4096]
    mskb = arD[:, 4096:4096 + NMC * 512]
    etile = [arD[:, 12800 + i * 512:12800 + (i + 1) * 512] for i in range(8)]
    cd_sb = arA[:, 0:8192]
    sd_sb = arA[:, 8192:16384]
    yT = arA[:, 16384:32768]
    wout_c = [arD[:, 16896 + i * 1024:16896 + (i + 1) * 1024] for i in range(2)]
    abA_t = [arD[:, 8192 + i * 1024:8192 + (i + 1) * 1024] for i in range(2)]
    abB_t = [arD[:, 10240 + i * 1024:10240 + (i + 1) * 1024] for i in range(2)]
    gre_sb = arA[:, 0:8192]
    gim_sb = arA[:, 8192:16384]
    gren_sb = arA[:, 16384:24576]
    yev_t = [arA[:, 24576 + i * 2048:24576 + (i + 1) * 2048] for i in range(2)]
    zre_t = [arA[:, 28672 + i * 1024:28672 + (i + 1) * 1024] for i in range(2)]
    zim_t = [arA[:, 30720 + i * 1024:30720 + (i + 1) * 1024] for i in range(2)]
    s3re_t = [arA[0:64, 24576 + i * 4096:24576 + (i + 1) * 4096] for i in range(2)]
    s3im_t = [arA[0:64, i * 4096:(i + 1) * 4096] for i in range(2)]
    f64_sb = arA[0:64, 32768:32800]
    w2s_v = arD[:, 4096:8192]
    gus_v = arD[:, 12288:20480]
    xt_f = [arE[:, i * 1024:(i + 1) * 1024] for i in range(2)]
    sqscr = arE[:, 2048:3072]
    bc_sb = [arE[:, 2048 + i * 512:2048 + (i + 1) * 512] for i in range(4)]
    rs_r = [arE[0:1, 4096 + i * 512:4096 + (i + 1) * 512] for i in range(4)]
    gates = arE[:, 3072:3200]
    glog = arE[:, 3200:3264]
    stat = arE[:, 3264:3328]
    xf_t = [arE[:, 2048 + i * 1024:2048 + (i + 1) * 1024] for i in range(2)]
    ot_f = [arE[:, 4096 + i * 1024:4096 + (i + 1) * 1024] for i in range(2)]
    s3e_t = [arE[0:16, 4096 + i * 1024:4096 + (i + 1) * 1024] for i in range(2)]
    convb_c = cst[:, 0:8]
    bout_c = cst[:, 8:16]
    ones_f = cst[0:1, 16:80]
    ident_f = cst[:, 128:256]

    TT = {k: taps[k].ap() for k in taps}

    S.dma("gpsimd", cbb_bf, dg("conv_bb")[:, :], sem="gc")
    S.dma("sync", ident_f, dg("ident")[:, :], sem="cst")
    S.dma_pcf("gpsimd", gwt_bf, dg("gate_wt"), c=8, sem="gc")
    for bp0 in (0, 64):
        S.dma("gpsimd",
              convw_sb[bp0:bp0 + 64, :].rearrange("i (g t o) -> i g t o", g=16, t=4),
              dg("conv_tw").rearrange("g t i o -> i g t o"), sem="gc")
    m_cg = S.val("gc")
    S.dma("sync", convb_c, dg("conv_b").rearrange("(c p) -> p c", p=128), slow=True,
          sem="cst")
    S.dma("sync", bout_c, dg("b_out").rearrange("(c p) -> p c", p=128), slow=True,
          sem="cst")
    m_cs = S.val("cst")
    m_ones_f = S.emit("gpsimd", lambda e: e.memset(ones_f, 1.0), inc=("gS", 1))

    x_ap = dg("x_sl")
    m_ld = [0] * NT; m_sq = [0] * NT; m_var = [0] * NT; m_rstd = [0] * NT
    m_nb = [0] * NT; m_xn = [0] * NT; m_tr = [0] * NT; m_tev = [0] * NT
    inv_d = 1.0 / D
    for i in range(NT):
        buf = xt_f[i % 2]
        w = [("tS", m_tr[i - 2])] if i >= 2 else []
        m_ld[i] = S.dma("sync", buf, x_ap[i * 128:(i + 1) * 128, :], waits=w,
                        sem="x%d" % (i % 2))
        c0 = stat[:, (i % 4) * 8:(i % 4) * 8 + 8]
        s1, s2, mean, ex2, var, rstd, nb, tmp = (c0[:, j:j + 1] for j in range(8))
        mv = S.emit("vector",
                    lambda e, b=buf, o=s1: e.tensor_reduce(o, b, axis=AX.X, op=ALU.add),
                    waits=[("x%d" % (i % 2), m_ld[i])], inc=("vS", 1))
        m_sq[i] = S.emit("scalar",
                         lambda e, b=buf, o=s2: e.activation(sqscr, b, AF.Square, accum_out=o),
                         waits=[("x%d" % (i % 2), m_ld[i])], inc=("sS", 1))
        mv = S.emit("vector",
                    lambda e, a=mean, b=s1: e.tensor_scalar(a, b, inv_d, None, op0=ALU.mult),
                    waits=[("sS", m_sq[i]), ("vS", mv)], inc=("vS", 1))
        mv = S.emit("vector", lambda e, a=ex2, b=s2:
                    e.tensor_scalar(a, b, inv_d, 1e-5, op0=ALU.mult, op1=ALU.add),
                    waits=[("vS", mv)], inc=("vS", 1))
        mv = S.emit("vector", lambda e, a=tmp, b=mean: e.tensor_tensor(a, b, b, op=ALU.mult),
                    waits=[("vS", mv)], inc=("vS", 1))
        m_var[i] = S.emit("vector",
                          lambda e, a=var, b=ex2, c=tmp: e.tensor_tensor(a, b, c, op=ALU.subtract),
                          waits=[("vS", mv)], inc=("vS", 1))
        ms = S.emit("scalar", lambda e, a=tmp, b=var: e.activation(a, b, AF.Ln),
                    waits=[("vS", m_var[i])], inc=("sS", 1))
        m_rstd[i] = S.emit("scalar",
                           lambda e, a=rstd, b=tmp: e.activation(a, b, AF.Exp, scale=-0.5),
                           waits=[("sS", ms)], inc=("sS", 1))
        mv = S.emit("vector", lambda e, a=tmp, b=mean, c=rstd: e.tensor_tensor(a, b, c, op=ALU.mult),
                    waits=[("sS", m_rstd[i]), ("vS", m_var[i])], inc=("vS", 1))
        m_nb[i] = S.emit("vector",
                         lambda e, a=nb, b=tmp: e.tensor_scalar(a, b, -1.0, None, op0=ALU.mult),
                         waits=[("vS", mv)], inc=("vS", 1))
        wx = [("vS", m_nb[i])]
        m_xn[i] = S.emit("scalar",
                         lambda e, b=buf, scc=rstd, bi=nb:
                         e.activation(b, b, AF.Identity, bias=bi, scale=scc),
                         waits=wx, inc=("sS", 1))
        for j in range(8):
            w2 = []
            if j == 0:
                w2 = [("sS", m_xn[i]), ("cst", m_cs)]
                if i >= 1:
                    w2.append(("vS", m_tev[i - 1]))
            m_tr[i] = S.emit("tensor",
                             lambda e, j=j, src=buf:
                             e.transpose(P_B[:, j * 128:(j + 1) * 128],
                                         src[:, j * 128:(j + 1) * 128], ident_f),
                             waits=w2, inc=("tS", 1) if j == 7 else None)
        def evac_fn(e, i=i):
            out_ap = xnT.rearrange("p (c t) -> p c t", c=8)[:, :, i * 128:(i + 1) * 128]
            return e.tensor_copy(out_ap,
                                 P_B[:, 0:1024].rearrange("p (c t) -> p c t", c=8))
        m_tev[i] = S.emit("vector", evac_fn, waits=[("tS", m_tr[i])], inc=("vS", 1))
    m_xnT = m_tev[NT - 1]
    if "xnT" in TT:
        S.dma("sync", TT["xnT"], xnT[:, :], waits=[("vS", m_xnT)], sem="tap")

    if PHASE_LIM < 1:
        S.terminal()
        return
    m_wq = [0] * 24
    m_qseg = [[0] * NSEG for _ in range(24)]
    m_qev = [[0] * NSEG for _ in range(24)]
    m_park = [0] * 8
    m_convk = [0] * 24
    m_kcev = {-2: 0, -1: 0}
    m_vcev = {}
    wqkv_ap = dg("wqkv")
    for fc in range(2):
        m_wq[fc] = S.dma_pcf("gpsimd", wq_b[fc % 2],
                             wqkv_ap[:, fc * 128:(fc + 1) * 128], c=8,
                             sem="wq%d" % (fc % 2))
    for fc in range(24):
        cb = chunkb[fc % 3]
        pw = []
        if fc >= 3:
            pfc = fc - 3
            pw = [("pk%d" % (pfc % 3), m_park[pfc])] if pfc < 8 else [("tS", m_convk[pfc])]
        for seg in range(NSEG):
            for dc in range(8):
                wts = []
                if dc == 0:
                    wts = [("wq%d" % (fc % 2), m_wq[fc]), ("vS", m_xnT)]
                    if fc >= 1:
                        wts.append(("vS", m_qev[fc - 1][seg]))
                    if seg == 0:
                        wts += pw
                m_qseg[fc][seg] = S.emit(
                    "tensor",
                    lambda e, seg=seg, dc=dc, fc=fc:
                    e.matmul(P_A[:, seg * 512:(seg + 1) * 512],
                             wq_b[fc % 2][:, dc * 128:(dc + 1) * 128],
                             xnT[:, dc * 2048 + seg * 512:dc * 2048 + (seg + 1) * 512],
                             start=(dc == 0), stop=(dc == 7)),
                    waits=wts, inc=("tS", 1) if dc == 7 else None)
            m_qev[fc][seg] = S.emit(
                "vector",
                lambda e, cb=cb, seg=seg:
                e.tensor_copy(cb[:, seg * 512:(seg + 1) * 512],
                              P_A[:, seg * 512:(seg + 1) * 512]),
                waits=[("tS", m_qseg[fc][seg])], inc=("vS", 1))
        if fc + 2 < 24:
            m_wq[fc + 2] = S.dma_pcf(
                "gpsimd", wq_b[fc % 2], wqkv_ap[:, (fc + 2) * 128:(fc + 3) * 128],
                c=8, waits=[("tS", m_qseg[fc][NSEG - 1])], sem="wq%d" % (fc % 2))
        if fc < 8:
            m_park[fc] = S.dma("sync", dg("qt_dram")[fc * 128:(fc + 1) * 128, :], cb[:, :],
                               waits=[("vS", m_qev[fc][NSEG - 1])],
                               sem="pk%d" % (fc % 3))
        elif fc < 16:
            cc = fc - 8
            for gl in range(2):
                g = 2 * cc + gl
                bp = (g % 2) * 64
                pbv = P_B[0:64, (g % 2) * 512:(g % 2) * 512 + 512]
                for tau in range(CF):
                    wts = []
                    if tau == 0:
                        wts = [("vS", m_qev[fc][NSEG - 1]), ("sS", m_kcev[g - 2])]
                        if g < 2:
                            wts += [("vS", m_tev[NT - 1]), ("gc", m_cg)]
                    m_convk[fc] = S.emit(
                        "tensor",
                        lambda e, g=g, tau=tau, cb=cb, pbv=pbv, bp=bp:
                        e.matmul(pbv,
                                 convw_sb[bp:bp + 64, (g * 4 + tau) * 64:(g * 4 + tau + 1) * 64],
                                 cb[bp:bp + 64, tau:2048:4],
                                 start=(tau == 0), stop=(tau == 3)),
                        waits=wts, inc=("tS", 1) if tau == 3 else None)
                m_kcev[g] = S.emit(
                    "scalar",
                    lambda e, g=g, pbv=pbv, bp=bp:
                    e.activation(kcT_l[:, g * 512:(g + 1) * 512], pbv, AF.Identity,
                                 bias=convb_c[bp:bp + 64, (g // 2):(g // 2) + 1]),
                    waits=[("tS", m_convk[fc]), ("cst", m_cs)], inc=("sS", 1))
        else:
            cc = fc - 16
            for gl in range(2):
                g = 2 * cc + gl
                bp = (g % 2) * 64
                pbv = P_B[:, (g % 2) * 512:(g % 2) * 512 + 64]
                for jc in range(4):
                    for tau in range(CF):
                        wts = []
                        if tau == 0:
                            wts = [("vS", m_vcev.get((g, jc - 1),
                                                     m_vcev.get((g - 2, 3), 0)))]
                            if jc == 0:
                                wts.append(("vS", m_qev[fc][NSEG - 1]))
                                if g < 2:
                                    wts += [("sS", m_kcev[15]), ("gc", m_cg)]
                        m_convk[fc] = S.emit(
                            "tensor",
                            lambda e, g=g, tau=tau, jc=jc, cb=cb, pbv=pbv, bp=bp:
                            e.matmul(pbv,
                                     cb[bp:bp + 64, jc * 512 + tau:jc * 512 + tau + 509:4],
                                     convw_sb[bp:bp + 64,
                                              (g * 4 + tau) * 64:(g * 4 + tau + 1) * 64],
                                     start=(tau == 0), stop=(tau == 3)),
                            waits=wts, inc=("tS", 1) if tau == 3 else None)
                    m_vcev[(g, jc)] = S.emit(
                        "vector",
                        lambda e, g=g, jc=jc, pbv=pbv:
                        e.tensor_copy(vc_l4[:, jc, g, 0:64], pbv),
                        waits=[("tS", m_convk[fc])], inc=("vS", 1))
    m_vcb = 0
    for jc in range(4):
        m_vcb = S.emit("vector",
                       lambda e, jc=jc:
                       e.tensor_tensor(vc_l4[:, jc, :, 0:64],
                                       vc_l4[:, jc, :, 0:64],
                                       cbb_bf.rearrange("p (h w) -> p h w", h=H),
                                       op=ALU.add),
                       waits=[("vS", m_vcev[(15, jc)]), ("gc", m_cg)], inc=("vS", 1))
        m_vcb = S.emit("vector",
                       lambda e, jc=jc: e.memset(vc_l4[:, jc, :, 64:65], 1.0),
                       waits=[("vS", m_vcb)], inc=("vS", 1))
    if "kcT" in TT:
        S.dma("sync", TT["kcT"], kcT_l[:, :], waits=[("sS", m_kcev[15])], sem="tap")
    if "vcL" in TT:
        S.dma("sync", TT["vcL"], vc_l[:, :], waits=[("vS", m_vcb)], sem="tap")

    kc_dst = dg("cc1_in")[0:512, :].rearrange("x d -> (x d)") \
        .rearrange("(g o j) -> o g j", o=64, g=16)
    m_kcb = S.dma("sync", kc_dst,
                  kcT_l.rearrange("o (g j) -> o g j", g=16),
                  waits=[("sS", m_kcev[15])], sem="bnc")
    m_b2 = S.dma("sync",
                 dg("cc1_in")[512:1032, :].rearrange("x d -> (x d)")
                 .rearrange("(c q w) -> q c w", q=128, c=4),
                 vc_l.rearrange("q (c w) -> q c w", c=4), waits=[("vS", m_vcb)],
                 sem="bnc")

    def cc1_fn(e):
        return e.collective_compute(
            "AllGather", ALU.bypass,
            replica_groups=[[0, 1, 2, 3], [4, 5, 6, 7]],
            ins=[dg("cc1_in")], outs=[dg("cc1_out")])
    S.emit("gpsimd", cc1_fn, waits=[("bnc", m_b2)], inc=("c1", 1))

    if PHASE_LIM < 2:
        S.terminal()
        return
    m_gl = [0] * NT
    m_gev = [0] * NT
    for tt in range(NT):
        for dc in range(8):
            wts = []
            if dc == 0:
                wts = [("gc", m_cg), ("vS", m_xnT)]
                if tt >= 2:
                    wts.append(("vS", m_gev[tt - 2]))
                if tt == 0:
                    wts += [("sS", m_kcev[15]), ("vS", m_vcb)]
            m_gl[tt] = S.emit(
                "tensor",
                lambda e, tt=tt, dc=dc:
                e.matmul(P_B[:, 512 + (tt % 2) * 512:512 + (tt % 2) * 512 + 8],
                         xnT[:, dc * 2048 + tt * 128:dc * 2048 + (tt + 1) * 128],
                         gwt_bf[:, dc * 8:(dc + 1) * 8],
                         start=(dc == 0), stop=(dc == 7)),
                waits=wts, inc=("tS", 1) if dc == 7 else None)
        c0 = glog[:, (tt % 2) * 32:(tt % 2) * 32 + 32]
        lg = c0[:, 0:8]; t2 = c0[:, 8:16]
        l1 = c0[:, 16:17]; nl1 = c0[:, 17:18]; l2 = c0[:, 18:19]
        sm = c0[:, 19:20]; rc = c0[:, 20:21]; ex = c0[:, 24:32]
        t3 = c0[:, 24:32]
        nk = stat[:, (tt % 2) * 8:(tt % 2) * 8 + 8]
        pbl = P_B[:, 512 + (tt % 2) * 512:512 + (tt % 2) * 512 + 8]
        mv = S.emit("vector", lambda e, lg=lg, pbl=pbl: e.tensor_copy(lg, pbl),
                    waits=[("tS", m_gl[tt])], inc=("vS", 1))
        mv = S.emit("vector",
                    lambda e, l1=l1, lg=lg: e.tensor_reduce(l1, lg, axis=AX.X, op=ALU.max),
                    waits=[("vS", mv)], inc=("vS", 1))
        mv = S.emit("vector", lambda e, t2=t2, lg=lg, l1=l1:
                    e.tensor_scalar(t2, lg, l1, None, op0=ALU.is_lt),
                    waits=[("vS", mv)], inc=("vS", 1))
        mv = S.emit("vector", lambda e, nk=nk, t2=t2:
                    e.tensor_scalar(nk, t2, 1e9, -1e9, op0=ALU.mult, op1=ALU.add),
                    waits=[("vS", mv)], inc=("vS", 1))
        mv = S.emit("vector", lambda e, t3=t3, t2=t2, lg=lg:
                    e.tensor_tensor(t3, lg, t2, op=ALU.mult),
                    waits=[("vS", mv)], inc=("vS", 1))
        mv = S.emit("vector", lambda e, t3=t3, nk=nk:
                    e.tensor_tensor(t3, t3, nk, op=ALU.add),
                    waits=[("vS", mv)], inc=("vS", 1))
        mv = S.emit("vector",
                    lambda e, l2=l2, t3=t3: e.tensor_reduce(l2, t3, axis=AX.X, op=ALU.max),
                    waits=[("vS", mv)], inc=("vS", 1))
        m_v1 = S.emit("vector",
                      lambda e, nl1=nl1, l1=l1: e.tensor_scalar(nl1, l1, -1.0, None, op0=ALU.mult),
                      waits=[("vS", mv)], inc=("vS", 1))
        m_s1 = S.emit("scalar",
                      lambda e, ex=ex, lg=lg, nl1=nl1: e.activation(ex, lg, AF.Exp, bias=nl1),
                      waits=[("vS", m_v1)], inc=("sS", 1))
        mv = S.emit("vector",
                    lambda e, sm=sm, ex=ex: e.tensor_reduce(sm, ex, axis=AX.X, op=ALU.add),
                    waits=[("sS", m_s1), ("vS", m_v1)], inc=("vS", 1))
        mv = S.emit("vector", lambda e, rc=rc, sm=sm: e.reciprocal(rc, sm),
                    waits=[("vS", mv)], inc=("vS", 1))
        mv = S.emit("vector", lambda e, rc=rc: e.tensor_scalar(rc, rc, 0.5, None, op0=ALU.mult),
                    waits=[("vS", mv)], inc=("vS", 1))
        mv = S.emit("vector", lambda e, t2=t2, lg=lg, l2=l2:
                    e.tensor_scalar(t2, lg, l2, None, op0=ALU.is_ge),
                    waits=[("vS", mv)], inc=("vS", 1))
        mv = S.emit("vector", lambda e, t2=t2, ex=ex: e.tensor_tensor(t2, t2, ex, op=ALU.mult),
                    waits=[("vS", mv)], inc=("vS", 1))
        m_gev[tt] = S.emit("vector",
                           lambda e, tt=tt, t2=t2, rc=rc:
                           e.tensor_scalar(gates[:, tt * 8:(tt + 1) * 8], t2, rc, None,
                                           op0=ALU.mult),
                           waits=[("vS", mv)], inc=("vS", 1))
    m_gates = m_gev[NT - 1]
    if "gates" in TT:
        S.dma("sync", TT["gates"], gates, waits=[("vS", m_gates)], sem="tap")

    if PHASE_LIM < 3:
        S.terminal()
        return
    m_ffn0 = S.emit("vector", lambda e: e.memset(ffn[:, :], 0.0), inc=("vS", 1))

    state = {"last_phB": m_gl[NT - 1], "ev_idx": 0, "phA_q": 0,
             "moe_prev_silu": [0, 0], "moe_prev_gu": [0, 0],
             "last_phA": 0, "prev_w13_consumed": [m_gl[NT - 1], m_gl[NT - 1]],
             "w2_extra": [("bnc", m_b2), ("vS", m_vcb)],
             "w13_extra": [],
             "silu_extra": [("bnc", m_b2)]}
    m_ffn_ev = {-3: 0, -2: 0, -1: 0}

    def moe_pass(f_hid, w1_ap, w3_ap, w2_ap, gate_col, gate_const,
                 gu_v=None, w2_v=None, w13_v=None):
        gu_l = gu_v if gu_v is not None else gu
        w2_l = w2_v if w2_v is not None else w2half
        w13_l = w13_v if w13_v is not None else w13_b
        nfc = f_hid // 128
        PS = (P_A, P_B)
        mb01 = [0, 0]
        for fc0 in range(min(2, nfc)):
            S.dma_pcf("gpsimd", w13_l[fc0][0],
                      w1_ap[:, fc0 * 128:(fc0 + 1) * 128], c=8,
                      waits=[("tS", state["prev_w13_consumed"][fc0])]
                      + state["w13_extra"],
                      sem="wq%d" % fc0)
            state["w13_extra"] = []
            mb01[fc0] = S.dma_pcf("gpsimd", w13_l[fc0][1],
                                  w1_ap if False else w3_ap[:, fc0 * 128:(fc0 + 1) * 128],
                                  c=8, sem="wq%d" % fc0)
        m_w2 = S.dma_pcf("gpsimd", w2_l[:, 0:nfc * 1024], w2_ap, c=nfc,
                         waits=[("tS", state["last_phB"])] + state["w2_extra"],
                         sem="w2s")
        state["w2_extra"] = []
        m_gu_last = 0
        m_phA = {}
        mbv = dict(enumerate(mb01))
        prev_silu = state["moe_prev_silu"]
        prev_gu = state["moe_prev_gu"]
        pg = m_ffn_ev.get(state["ev_idx"] - 1, 0)
        prev_gu[0] = max(prev_gu[0], pg)
        prev_gu[1] = max(prev_gu[1], pg)
        for fc in range(nfc):
            mb = mbv[fc]
            m_w1 = 0
            for dc in range(8):
                for sg in range(4):
                    wts = []
                    if dc == 0 and sg == 0:
                        wts = [("wq%d" % (fc % 2), mb), ("vS", m_gates),
                               ("sS", prev_silu[0]), ("vS", prev_gu[0])]
                    m_w1 = S.emit(
                        "tensor",
                        lambda e, sg=sg, dc=dc, fc=fc:
                        e.matmul(P_A[:, sg * 512:(sg + 1) * 512],
                                 w13_l[fc % 2][0][:, dc * 128:(dc + 1) * 128],
                                 xnT[:, dc * 2048 + sg * 512:dc * 2048 + (sg + 1) * 512],
                                 start=(dc == 0), stop=(dc == 7)),
                        waits=wts, inc=("tS", 1) if (sg == 3 and dc == 7) else None)
            prev_silu[0] = S.emit(
                "scalar",
                lambda e, fc=fc:
                e.activation(gu_l[:, fc * 2048:(fc + 1) * 2048], P_A[:, :], AF.Silu),
                waits=[("tS", m_w1)] + state["silu_extra"], inc=("sS", 1))
            state["silu_extra"] = []
            mlast = 0
            for dc in range(8):
                for sg in range(4):
                    wts = []
                    if dc == 0 and sg == 0:
                        wts = [("vS", prev_gu[1])]
                    mlast = S.emit(
                        "tensor",
                        lambda e, sg=sg, dc=dc, fc=fc:
                        e.matmul(P_B[:, sg * 512:(sg + 1) * 512],
                                 w13_l[fc % 2][1][:, dc * 128:(dc + 1) * 128],
                                 xnT[:, dc * 2048 + sg * 512:dc * 2048 + (sg + 1) * 512],
                                 start=(dc == 0), stop=(dc == 7)),
                        waits=wts, inc=("tS", 1) if (sg == 3 and dc == 7) else None)
            m_phA[fc] = mlast
            state["last_phA"] = mlast
            prev_gu[1] = S.emit(
                "vector",
                lambda e, fc=fc:
                e.tensor_tensor(gu_l[:, fc * 2048:(fc + 1) * 2048],
                                gu_l[:, fc * 2048:(fc + 1) * 2048],
                                P_B[:, :], op=ALU.mult),
                waits=[("sS", prev_silu[0]), ("tS", mlast)], inc=("vS", 1))
            state["prev_w13_consumed"][fc % 2] = m_phA[fc]
            if fc + 2 < nfc:
                S.dma_pcf("gpsimd", w13_l[fc % 2][0],
                          w1_ap[:, (fc + 2) * 128:(fc + 3) * 128], c=8,
                          waits=[("tS", m_phA[fc])], sem="wq%d" % (fc % 2))
                mbv[fc + 2] = S.dma_pcf("gpsimd", w13_l[fc % 2][1],
                                        w3_ap[:, (fc + 2) * 128:(fc + 3) * 128],
                                        c=8, sem="wq%d" % (fc % 2))
            m_gu_last = prev_gu[1]
        m_gu_all = max(prev_gu[0], prev_gu[1])
        for tt in range(NT):
            ev = state["ev_idx"]
            bs = PS[tt % 2]
            mlast = 0
            for fc in range(nfc):
                for do in range(2):
                    wts = []
                    if fc == 0 and do == 0:
                        wts = [("vS", m_gu_all), ("w2s", m_w2),
                               ("vS", m_ffn_ev[ev - 3]),
                               ("sS", prev_silu[0])]
                    mlast = S.emit(
                        "tensor",
                        lambda e, tt=tt, do=do, fc=fc, bs=bs, nfc=nfc:
                        e.matmul(bs[:, do * 512:(do + 1) * 512],
                                 gu_l[:, fc * 2048 + tt * 128:fc * 2048 + (tt + 1) * 128],
                                 w2_l[:, fc * 1024 + do * 512:fc * 1024 + (do + 1) * 512],
                                 start=(fc == 0), stop=(fc == nfc - 1)),
                        waits=wts, inc=("tS", 1) if fc == nfc - 1 else None)
            state["last_phB"] = mlast
            gc = gate_col(tt) if gate_col is not None else gate_const
            for do in range(2):
                m_ffn_ev[state["ev_idx"]] = S.emit(
                    "vector",
                    lambda e, tt=tt, do=do, bs=bs, gc=gc:
                    e.scalar_tensor_tensor(
                        ffn[:, tt * 1024 + do * 512:tt * 1024 + (do + 1) * 512],
                        bs[:, do * 512:(do + 1) * 512], gc,
                        ffn[:, tt * 1024 + do * 512:tt * 1024 + (do + 1) * 512],
                        op0=ALU.mult, op1=ALU.add),
                    waits=[("tS", mlast), ("vS", m_ffn0)], inc=("vS", 1))
                state["ev_idx"] += 1

    for e_i in range(NE):
        for hh in range(2):
            moe_pass(2048,
                     dg("w1_e")[e_i][:, hh * 2048:(hh + 1) * 2048],
                     dg("w3_e")[e_i][:, hh * 2048:(hh + 1) * 2048],
                     dg("w2_e")[e_i][hh * 2048:(hh + 1) * 2048, :],
                     lambda tt, e_i=e_i: gates[:, tt * 8 + e_i:tt * 8 + e_i + 1], None)
    m_moe_t = state["last_phB"]
    m_moe_tA = state["last_phA"]
    if "ffn" in TT:
        S.dma("sync", TT["ffn"], ffn[:, :], waits=[("vS", m_moe_v)], sem="tap")

    if PHASE_LIM < 4:
        S.terminal()
        return
    kcF4 = kcF.rearrange("p (c m) -> p c m", c=8)
    m_z1 = S.emit("vector", lambda e: e.memset(kcF4[:, :, 2049:M], 0.0),
                  waits=[("tS", m_moe_t), ("vS", m_ffn_ev[state["ev_idx"] - 1])],
                  inc=("vS", 1))
    m_z2 = S.emit("vector", lambda e: e.memset(vcF4[:, NMC - 1, :, :], 0.0),
                  waits=[("vS", m_z1)], inc=("vS", 1))
    m_onesv = S.emit("vector",
                     lambda e: e.memset(vcF4[0:1, 0, :, 64:65], 1.0),
                     waits=[("vS", m_z2)], inc=("vS", 1))
    ccv = dg("cc1_out").rearrange("(r x) d -> r (x d)", r=4)
    vcg_r = [ccv[r][512 * 1024:1032 * 1024].rearrange("(x w) -> x w", x=512)
             for r in range(4)]
    m_kcl = 0
    for r in range(4):
        kcr = ccv[r][0:512 * 1024].rearrange("(f j) -> f j", f=1024)
        for dc in range(8):
            wts = [("c1", 1), ("tS", m_moe_t),
                   ("vS", m_ffn_ev[state["ev_idx"] - 1])] if (r == 0 and dc == 0) else []
            m_kcl = S.dma("sync", kcF[:, dc * M + 1 + r * 512:dc * M + 1 + (r + 1) * 512],
                          kcr[dc * 128:(dc + 1) * 128, :], waits=wts, sem="kcl")
    m_vcl = 0
    for cm in range(NMC):
        lo = max(cm * 128 - 1, 0)
        hi = min(cm * 128 + 127, 2048)
        dp0 = 1 if cm == 0 else 0
        wts = [("c1", 1), ("tS", m_moe_t),
               ("vS", m_ffn_ev[state["ev_idx"] - 1])] if cm == 0 else \
            ([("vS", m_onesv)] if cm == NMC - 1 else [])
        j = lo
        while j < hi:
            r = j // 512
            je = min(hi, (r + 1) * 512)
            n = je - j
            dp = dp0 + (j - lo)
            m_vcl = S.dma("sync", vcF4[dp:dp + n, cm, :, :],
                          vcg_r[r][j - r * 512:je - r * 512, :]
                          .rearrange("x (h w) -> x h w", h=H, w=65),
                          waits=wts, sem="vcl")
            wts = []
            j = je
    m_nk = 0
    for dc in range(8):
        wts = [("tS", m_moe_t), ("vS", m_ffn_ev[state["ev_idx"] - 1])] \
            if dc == 0 else []
        m_nk = S.dma("gpsimd", kcF[:, dc * M:dc * M + 1],
                     dg("null_k")[dc * 128:(dc + 1) * 128].rearrange("(p o) -> p o", o=1),
                     waits=wts, slow=True, sem="nul")
    m_nv = S.dma("gpsimd", vcF4[0:1, 0, :, 0:64],
                 dg("null_v").rearrange("(o h w) -> o h w", o=1, h=H),
                 waits=[("vS", m_onesv)], sem="nul")

    m_qp = [0] * NSEG
    m_mp = [0] * NSEG
    m_oev = {}
    last_exp = 0
    exp_slot = [0, 0, 0, 0]
    last_av = {i: 0 for i in range(8)}
    last_msk = 0
    last_sc_mm = 0
    mv_prev = 0
    oev_prev = 0
    for p in range(NSEG):
        n_p = 14 + p
        wq = [("tS", m_moe_tA)] if p == 0 else [("tS", last_sc_mm)]
        m_qp[p] = S.dma_pcf("sync", qTs,
                            dg("qt_dram")[:, p * 512:(p + 1) * 512], c=8, waits=wq,
                            sem="qts")
        wm = [("tS", m_moe_t)] if p == 0 else [("vS", last_msk)]
        m_mp[p] = S.dma_pcf("sync", mskb[:, 0:n_p * 512],
                            dg("masks_in")[MASK_OFF[p] * 128:(MASK_OFF[p] + n_p) * 128, :],
                            c=n_p, waits=wm, sem="msk")
        for hg in range(H // 4):
            heads = [4 * hg + i for i in range(4)]
            for cm in range(n_p):
                eb = [4 * (cm % 2) + i for i in range(4)]
                for i, h in enumerate(heads):
                    bp = (h % 2) * 64
                    wts = [("sS", exp_slot[i])]
                    if i == 0 and cm == 0:
                        if hg == 0:
                            wts += [("qts", m_qp[p]), ("kcl", m_kcl), ("nul", m_nv)]
                        wts.append(("vS", mv_prev))
                    last_sc_mm = S.emit(
                        "tensor",
                        lambda e, h=h, i=i, cm=cm, bp=bp:
                        e.matmul(P_A[:, i * 512:(i + 1) * 512],
                                 kcF[bp:bp + 64,
                                     (h // 2) * M + cm * 128:(h // 2) * M + (cm + 1) * 128],
                                 qTs[bp:bp + 64, (h // 2) * 512:(h // 2) * 512 + 512],
                                 start=True, stop=True),
                        waits=wts, inc=("tS", 1))
                for i in range(4):
                    last_exp = exp_slot[i] = S.emit(
                        "scalar",
                        lambda e, i=i, eb=eb:
                        e.activation(etile[eb[i]], P_A[:, i * 512:(i + 1) * 512],
                                     AF.Exp, scale=float(D) ** -0.5),
                        waits=([("tS", last_sc_mm)] if i == 0 else [])
                        + [("tS", last_av[eb[i]])],
                        inc=("sS", 1))
                for i in range(4):
                    last_msk = S.emit(
                        "vector",
                        lambda e, i=i, cm=cm, eb=eb:
                        e.tensor_tensor(etile[eb[i]], etile[eb[i]],
                                        mskb[:, cm * 512:(cm + 1) * 512], op=ALU.mult),
                        waits=([("sS", last_exp), ("msk", m_mp[p])] if i == 0 else []),
                        inc=("vS", 1))
                for i, h in enumerate(heads):
                    wav = []
                    if i == 0:
                        wav = [("vS", last_msk)]
                        if cm == 0:
                            wav += [("vcl", m_vcl), ("nul", m_nv), ("vS", oev_prev)]
                    last_av[eb[i]] = S.emit(
                        "tensor",
                        lambda e, h=h, i=i, cm=cm, n_p=n_p, eb=eb:
                        e.matmul(P_B[0:65, i * 512:(i + 1) * 512],
                                 vcF[:, cm * 1040 + h * 65:cm * 1040 + h * 65 + 65],
                                 etile[eb[i]],
                                 start=(cm == 0), stop=(cm == n_p - 1)),
                        waits=wav, inc=("tS", 1))
            m_av_all = last_av[eb[3]]
            m_rc = 0
            for i in range(4):
                m_rc = S.emit("vector",
                              lambda e, i=i:
                              e.reciprocal(rs_r[i][:, :], P_B[64:65, i * 512:(i + 1) * 512]),
                              waits=([("tS", m_av_all), ("sS", last_exp)] if i == 0 else []),
                              inc=("vS", 1))
            m_bc = 0
            for i in range(4):
                m_bc = S.emit("tensor",
                              lambda e, i=i:
                              e.matmul(P_A[0:64, i * 512:(i + 1) * 512], ones_f,
                                       rs_r[i][:, :], start=True, stop=True),
                              waits=([("vS", m_rc), ("gS", m_ones_f)] if i == 0 else []),
                              inc=("tS", 1))
            mv = 0
            for i in range(4):
                mv = S.emit("vector",
                            lambda e, i=i:
                            e.tensor_copy(bc_sb[i][0:64, :], P_A[0:64, i * 512:(i + 1) * 512]),
                            waits=([("tS", m_bc), ("vS", m_rc)] if i == 0 else []),
                            inc=("vS", 1))
            mv_prev = mv
            for i, h in enumerate(heads):
                oev_prev = m_oev[(p, hg)] = S.emit(
                    "vector",
                    lambda e, p=p, h=h, i=i:
                    e.tensor_tensor(outT[(h % 2) * 64:(h % 2) * 64 + 64,
                                         (h // 2) * 2048 + p * 512:(h // 2) * 2048 + (p + 1) * 512],
                                    bc_sb[i][0:64, :], P_B[0:64, i * 512:(i + 1) * 512],
                                    op=ALU.mult),
                    waits=([("vS", mv)] if i == 0 else []), inc=("vS", 1))
    m_attn_v = m_oev[(NSEG - 1, H // 4 - 1)]
    m_attn_t = S.val("tS")
    if "outT" in TT:
        S.dma("sync", TT["outT"], outT[:, :], waits=[("vS", m_attn_v)], sem="tap")

    if PHASE_LIM < 5:
        S.terminal()
        return
    m_wo = {}
    for fcc in range(2):
        m_wo[fcc] = S.dma_pcf("gpsimd", wout_c[fcc],
                              dg("w_out")[:, fcc * 128:(fcc + 1) * 128], c=8,
                              waits=[("tS", m_moe_t)] if fcc == 0 else [],
                              sem="wo%d" % fcc)
    m_cd = S.dma_pcf("scalar", cd_sb, dg("cd_in"), c=8,
                     waits=[("tS", last_sc_mm)], sem="cds")
    m_sd = S.dma_pcf("scalar", sd_sb, dg("sd_in"), c=8, sem="cds")
    m_yev = {-4: 0, -3: 0, -2: 0, -1: 0}
    m_womm = {}
    for idx in range(32):
        fcc, ts = divmod(idx, NSEG)
        seg = idx % 4
        mlast = 0
        for dc in range(8):
            wts = []
            if dc == 0:
                wts = [("wo%d" % (fcc % 2), m_wo[fcc]), ("vS", m_attn_v),
                       ("sS", m_yev[idx - 4])]
            mlast = S.emit(
                "tensor",
                lambda e, fcc=fcc, ts=ts, dc=dc, seg=seg:
                e.matmul(P_A[:, seg * 512:(seg + 1) * 512],
                         wout_c[fcc % 2][:, dc * 128:(dc + 1) * 128],
                         outT[:, dc * 2048 + ts * 512:dc * 2048 + (ts + 1) * 512],
                         start=(dc == 0), stop=(dc == 7)),
                waits=wts, inc=("tS", 1) if dc == 7 else None)
        m_womm[fcc] = mlast
        if ts == NSEG - 1 and fcc + 2 < 8:
            m_wo[fcc + 2] = S.dma_pcf("gpsimd", wout_c[fcc % 2],
                                      dg("w_out")[:, (fcc + 2) * 128:(fcc + 3) * 128],
                                      c=8, waits=[("tS", mlast)],
                                      sem="wo%d" % (fcc % 2))
        m_yev[idx] = S.emit(
            "scalar",
            lambda e, fcc=fcc, ts=ts, seg=seg:
            e.activation(yT[:, fcc * 2048 + ts * 512:fcc * 2048 + (ts + 1) * 512],
                         P_A[:, seg * 512:(seg + 1) * 512], AF.Identity,
                         bias=bout_c[:, fcc:fcc + 1]),
            waits=[("tS", mlast)], inc=("sS", 1))
    m_yT = m_yev[31]
    if "yT" in TT:
        S.dma("sync", TT["yT"], yT[:, :], waits=[("sS", m_yT)], sem="tap")

    m_abd = 0
    m_abev = {}
    m_abdma = {}
    for tt in range(NT):
        for comp, msb, dst_row, abt in ((0, cd_sb, 0, abA_t),
                                        (1, sd_sb, 2048, abB_t)):
            for kdo in range(2):
                seg = (2 * comp + kdo) % 4
                mlast = 0
                for fcc in range(8):
                    wts = []
                    if fcc == 0:
                        wts = [("cds", m_sd), ("sS", m_yT),
                               ("vS", m_abev.get((tt - 1, comp, kdo), 0))]
                    mlast = S.emit(
                        "tensor",
                        lambda e, tt=tt, kdo=kdo, fcc=fcc, seg=seg, msb=msb:
                        e.matmul(P_A[:, seg * 512:(seg + 1) * 512],
                                 yT[:, fcc * 2048 + tt * 128:fcc * 2048 + (tt + 1) * 128],
                                 msb[:, fcc * 1024 + kdo * 512:fcc * 1024 + (kdo + 1) * 512],
                                 start=(fcc == 0), stop=(fcc == 7)),
                        waits=wts, inc=("tS", 1) if fcc == 7 else None)
                m_abev[(tt, comp, kdo)] = S.emit(
                    "vector",
                    lambda e, tt=tt, kdo=kdo, seg=seg, abt=abt:
                    e.tensor_copy(abt[tt % 2][:, kdo * 512:(kdo + 1) * 512],
                                  P_A[:, seg * 512:(seg + 1) * 512]),
                    waits=[("tS", mlast),
                           ("ab%d" % (tt % 2), m_abdma.get((tt - 2, 1), 0))],
                    inc=("vS", 1))
            m_abdma[(tt, comp)] = m_abd = S.dma(
                "sync", dg("cc2_in")[dst_row + tt * 128:dst_row + (tt + 1) * 128, :],
                abt[tt % 2][:, :], waits=[("vS", m_abev[(tt, comp, 1)])],
                sem="ab%d" % (tt % 2))
    if "ab" in TT:
        S.dma("sync", TT["ab"], dg("cc2_in")[:, :],
              waits=[("ab0", m_abdma[(14, 1)]), ("ab1", m_abdma[(15, 1)])], sem="tap")

    def cc2_fn(e):
        return e.collective_compute(
            "AllGather", ALU.bypass,
            replica_groups=[[0, 1, 2, 3], [4, 5, 6, 7]],
            ins=[dg("cc2_in")], outs=[dg("cc2_out")])
    S.emit("gpsimd", cc2_fn, waits=[("ab0", m_abdma[(14, 1)]),
                                    ("ab1", m_abdma[(15, 1)])], inc=("c2", 1))

    m_abv_last = m_abev[(15, 1, 1)]
    state["w13_extra"] = [("tS", S.val("tS")), ("vS", m_abv_last)]
    state["w2_extra"] = [("vS", m_abv_last)]
    state["silu_extra"] = [("vS", m_abv_last)]
    state["moe_prev_silu"] = [m_yev[31], m_yev[31]]
    state["moe_prev_gu"] = [m_abv_last, m_abv_last]
    m_ffn_ev[state["ev_idx"] - 3] = max(
        m_ffn_ev.get(state["ev_idx"] - 3, 0), m_abv_last)
    for n_i in range(NS):
        moe_pass(512, dg("w1_s")[n_i], dg("w3_s")[n_i], dg("w2_s")[n_i],
                 None, 1.0 / 16.0, gu_v=gus_v, w2_v=w2s_v)
    m_moe_v = m_ffn_ev[state["ev_idx"] - 1]
    m_shared_t = state["last_phB"]

    if PHASE_LIM < 6:
        S.terminal()
        return
    m_last_ab_mm = S.val("tS")
    m_gc = S.dma_pcf("scalar", gre_sb, dg("gre_in"), c=64,
                     waits=[("tS", m_last_ab_mm)], sem="cds")
    m_gc = S.dma_pcf("scalar", gim_sb, dg("gim_in"), c=64, sem="cds")
    m_gc = S.dma_pcf("scalar", gren_sb, dg("gren_in"), c=64, sem="cds")
    m_f64 = S.dma("sync", f64_sb[:, 0:16], dg("f64c_in")[:, :],
                  waits=[("tS", m_last_ab_mm)], sem="cst")
    m_f64 = S.dma("sync", f64_sb[:, 16:32], dg("f64s_in")[:, :], sem="cst")
    abg = dg("cc2_out").rearrange("(r two q a) d -> two a r q d", r=4, two=2, q=32)
    zre_v = abg[0]
    zim_v = abg[1]
    m_s1mm = {-2: 0, -1: 0}
    m_s1ev = {-2: 0, -1: 0}
    m_s1ev2 = {-2: 0, -1: 0}
    m_s1d = {}
    for a in range(64):
        wz = [("c2", 1)] if a == 0 else []
        if a >= 2:
            wz.append(("tS", m_s1mm[a - 2]))
        for r in range(4):
            S.dma("sync", zre_t[a % 2][r * 32:(r + 1) * 32, :], zre_v[a][r], waits=wz,
                  sem="zl%d" % (a % 2))
            wz = []
            m_zl = S.dma("sync", zim_t[a % 2][r * 32:(r + 1) * 32, :], zim_v[a][r],
                         sem="zl%d" % (a % 2))
        mlast = 0
        bs1 = P_A if a % 2 == 0 else P_B
        for part, po in ((0, 0), (1, 1024)):
            gA = gre_sb if part == 0 else gim_sb
            gB = gim_sb if part == 0 else gren_sb
            for kdo in range(2):
                wts = []
                if part == 0 and kdo == 0:
                    wts = [("zl%d" % (a % 2), m_zl), ("cds", m_gc),
                           ("vS", m_s1ev[a - 2]), ("sS", m_s1ev2[a - 2])]
                    if a < 2:
                        wts += [("vS", m_moe_v), ("sS", state["moe_prev_silu"][0]),
                                ("sS", state["moe_prev_silu"][1])]
                S.emit("tensor",
                       lambda e, a=a, kdo=kdo, gA=gA, po=po, bs1=bs1:
                       e.matmul(bs1[:, po + kdo * 512:po + (kdo + 1) * 512],
                                gA[:, a * 128:(a + 1) * 128],
                                zre_t[a % 2][:, kdo * 512:(kdo + 1) * 512],
                                start=True, stop=False),
                       waits=wts)
                mlast = S.emit("tensor",
                               lambda e, a=a, kdo=kdo, gB=gB, po=po, bs1=bs1:
                               e.matmul(bs1[:, po + kdo * 512:po + (kdo + 1) * 512],
                                        gB[:, a * 128:(a + 1) * 128],
                                        zim_t[a % 2][:, kdo * 512:(kdo + 1) * 512],
                                        start=False, stop=True),
                               inc=("tS", 1))
        m_s1mm[a] = mlast
        m_s1ev[a] = S.emit("vector",
                           lambda e, a=a, bs1=bs1:
                           e.tensor_copy(yev_t[a % 2][:, 0:1024], bs1[:, 0:1024]),
                           waits=[("tS", m_s1mm[a]),
                                  ("yw%d" % (a % 2), m_s1d.get(a - 2, 0))],
                           inc=("vS", 1))
        m_s1ev2[a] = S.emit("scalar",
                            lambda e, a=a, bs1=bs1:
                            e.activation(yev_t[a % 2][:, 1024:2048], bs1[:, 1024:2048], AF.Copy),
                            waits=[("tS", m_s1mm[a]),
                                   ("yw%d" % (a % 2), m_s1d.get(a - 2, 0))],
                            inc=("sS", 1))
        S.dma("scalar", dg("y_dram")[a * 128:(a + 1) * 128, :], yev_t[a % 2][:, 0:1024],
              waits=[("vS", m_s1ev[a]), ("sS", m_s1ev2[a])], sem="yw%d" % (a % 2))
        m_s1d[a] = S.dma("scalar", dg("y_dram")[8192 + a * 128:8192 + (a + 1) * 128, :],
                         yev_t[a % 2][:, 1024:2048], sem="yw%d" % (a % 2))

    if PHASE_LIM < 7:
        S.terminal()
        return
    ydv = dg("y_dram").rearrange("(two a k) d -> two a (k d)", two=2, a=64)
    m_s3mm = {-2: 0, -1: 0}
    m_s3ev = {-2: 0, -1: 0}
    m_s3d = {}
    ev3 = 0
    for kb in range(32):
        wz = []
        if kb == 0:
            wz = [("yw0", m_s1d[62]), ("yw1", m_s1d[63])]
        if kb < 2:
            wz.append(("tS", m_s1mm[63]))
        if kb >= 2:
            wz.append(("tS", m_s3mm[kb - 2]))
        S.dma("sync", s3re_t[kb % 2], ydv[0][:, kb * 4096:(kb + 1) * 4096], waits=wz,
              sem="s3l%d" % (kb % 2))
        m_s3l = S.dma("sync", s3im_t[kb % 2], ydv[1][:, kb * 4096:(kb + 1) * 4096],
                      sem="s3l%d" % (kb % 2))
        for i in range(4):
            bs3 = P_A if i % 2 == 0 else P_B
            pbase = 0
            mlast = 0
            for kdo in range(2):
                wts = []
                if i == 0 and kdo == 0:
                    wts = [("s3l%d" % (kb % 2), m_s3l), ("cst", m_f64),
                           ("vS", m_s1ev[63]), ("sS", m_s1ev2[63])]
                wts.append(("vS", m_s3ev[ev3 - 2]))
                S.emit("tensor",
                       lambda e, kb=kb, i=i, kdo=kdo, pbase=pbase, bs3=bs3:
                       e.matmul(bs3[0:16, pbase + kdo * 512:pbase + (kdo + 1) * 512],
                                f64_sb[:, 0:16],
                                s3re_t[kb % 2][:, i * 1024 + kdo * 512:i * 1024 + (kdo + 1) * 512],
                                start=True, stop=False),
                       waits=wts)
                mlast = S.emit("tensor",
                               lambda e, kb=kb, i=i, kdo=kdo, pbase=pbase, bs3=bs3:
                               e.matmul(bs3[0:16, pbase + kdo * 512:pbase + (kdo + 1) * 512],
                                        f64_sb[:, 16:32],
                                        s3im_t[kb % 2][:, i * 1024 + kdo * 512:
                                                       i * 1024 + (kdo + 1) * 512],
                                        start=False, stop=True),
                               inc=("tS", 1))
            m_s3mm[kb] = mlast
            kappa = kb * 4 + i
            m_s3ev[ev3] = S.emit(
                "vector",
                lambda e, i=i, pbase=pbase, bs3=bs3:
                e.tensor_copy(s3e_t[i % 2], bs3[0:16, pbase:pbase + 1024]),
                waits=[("tS", mlast),
                       ("xw%d" % (ev3 % 2), m_s3d.get(ev3 - 2, 0))], inc=("vS", 1))
            m_s3d[ev3] = S.dma("scalar",
                               dg("x_scr").rearrange("(l k) d -> k l d", l=16)[kappa],
                               s3e_t[i % 2],
                               waits=[("vS", m_s3ev[ev3])], sem="xw%d" % (ev3 % 2))
            ev3 += 1
    m_s3_all = m_s3d[ev3 - 1]
    if "xscr" in TT:
        S.dma("sync", TT["xscr"], dg("x_scr")[:, :],
              waits=[("xw0", m_s3d[ev3 - 2]), ("xw1", m_s3d[ev3 - 1])], sem="tap")

    if PHASE_LIM < 8:
        S.terminal()
        return
    m_fst = 0
    m_oadd = {-2: 0, -1: 0}
    m_xw_last = [m_s3d[ev3 - 2], m_s3d[ev3 - 1]]
    m_ost = {}
    for g in range(NT):
        wts = [("xw0", m_xw_last[0]), ("xw1", m_xw_last[1])] if g == 0 \
            else [("vS", m_oadd[g - 2])]
        S.dma("sync", xt_f[g % 2], x_ap[g * 128:(g + 1) * 128, :], waits=wts,
              sem="x%d" % (g % 2))
        m_fl = S.dma("sync", xf_t[g % 2], dg("x_scr")[g * 128:(g + 1) * 128, :],
                     sem="x%d" % (g % 2))
        mv = S.emit("vector",
                    lambda e, g=g:
                    e.tensor_tensor(ot_f[g % 2], xt_f[g % 2], xf_t[g % 2], op=ALU.add),
                    waits=[("x%d" % (g % 2), m_fl), ("vS", m_moe_v),
                           ("out%d" % (g % 2), m_ost.get(g - 2, 0))]
                    + ([("xw0", m_xw_last[0]), ("xw1", m_xw_last[1])] if g == 0 else []),
                    inc=("vS", 1))
        m_oadd[g] = S.emit(
            "vector",
            lambda e, g=g:
            e.tensor_tensor(ot_f[g % 2], ot_f[g % 2], ffn[:, g * 1024:(g + 1) * 1024],
                            op=ALU.add),
            waits=[("vS", mv)], inc=("vS", 1))
        m_ost[g] = m_fst = S.dma("sync", dg("y_out")[g * 128:(g + 1) * 128, :],
                                 ot_f[g % 2], waits=[("vS", m_oadd[g])],
                                 sem="out%d" % (g % 2))
    S.emit("sync", lambda e: e.nop(),
           waits=[("out0", m_ost[NT - 2]), ("out1", m_ost[NT - 1])])


_CACHE = {}


def kernel(**inputs):
    if "nc" not in _CACHE:
        _CACHE["nc"] = build_nc()
        _CACHE["const"] = _constants()
        _CACHE["masks"] = [_core_masks(s) for s in range(4)]
        _CACHE["f64"] = [_core_f64(s) for s in range(4)]
    nc = _CACHE["nc"]
    const = _CACHE["const"]

    g = {k: np.ascontiguousarray(np.asarray(v, dtype=np.float32))
         for k, v in inputs.items()}
    x = g["L0_attn_x"]
    conv_w = g["L0_attn_conv_w"]
    conv_tw = np.ascontiguousarray(
        conv_w.reshape(16, 64, 64, 4).transpose(0, 3, 2, 1))
    conv_b = g["L0_attn_conv_b"]
    shared = {
        "wqkv": g["L0_attn_Wqkv"],
        "conv_tw": conv_tw,
        "conv_b": conv_b,
        "conv_bb": np.ascontiguousarray(np.broadcast_to(conv_b, (128, D))),
        "null_k": np.ascontiguousarray(g["L0_attn_null_k"].reshape(D)),
        "null_v": np.ascontiguousarray(g["L0_attn_null_v"].reshape(D)),
        "w_out": g["L0_attn_W_out"],
        "b_out": g["L0_attn_b_out"],
        "gate_wt": np.ascontiguousarray(g["L0_ffn_gate_W"].T),
        "w1_e": g["L0_ffn_w1_e"], "w3_e": g["L0_ffn_w3_e"], "w2_e": g["L0_ffn_w2_e"],
        "w1_s": g["L0_ffn_w1_s"], "w3_s": g["L0_ffn_w3_s"], "w2_s": g["L0_ffn_w2_s"],
        "ident": const["ident"], "cd_in": const["cd_in"], "sd_in": const["sd_in"],
        "gre_in": const["gre_in"], "gim_in": const["gim_in"],
        "gren_in": const["gren_in"],
    }
    in_maps = []
    for c in range(8):
        b, s = divmod(c, 4)
        im = dict(shared)
        im["x_sl"] = np.ascontiguousarray(x[b, s * TL:(s + 1) * TL, :])
        im["masks_in"] = _CACHE["masks"][s]
        im["f64c_in"], im["f64s_in"] = _CACHE["f64"][s]
        in_maps.append(im)

    res = run_bass_kernel_spmd(nc, in_maps, core_ids=list(range(8)))
    out = np.empty((B, T, D), np.float32)
    for c in range(8):
        b, s = divmod(c, 4)
        out[b, s * TL:(s + 1) * TL, :] = res.results[c]["y_out"]
    if DEBUG_TAPS:
        _CACHE["taps"] = [{k: res.results[c]["tap_" + k] for k in DEBUG_TAPS}
                          for c in range(8)]
    return out


# version 09
# speedup vs baseline: 1.0565x; 1.0565x over previous
import contextlib
import numpy as np
import ml_dtypes
import concourse.bass as bass
import concourse.mybir as mybir
from concourse.bass_utils import run_bass_kernel_spmd

F32 = mybir.dt.float32
BF16 = mybir.dt.bfloat16
AF = mybir.ActivationFunctionType
ALU = mybir.AluOpType
AX = mybir.AxisListType
BF = ml_dtypes.bfloat16

B, T, D, H, CF, DH = 2, 8192, 1024, 16, 4, 64
TL, NT, NSEG = 2048, 16, 4
M, NMC = 2176, 17
FF, FS, NE, NS = 4096, 512, 8, 8

ENGINES = ("tensor", "vector", "scalar", "sync", "gpsimd")
DEBUG_TAPS = []
PHASE_LIM = 99
MASK_OFF = [0, 14, 29, 45]


class Sched:

    def __init__(self):
        self.ops = {e: [] for e in ENGINES}
        self.cnt = {}

    def val(self, sem):
        return self.cnt.get(sem, 0)

    def emit(self, eng, fn, waits=(), inc=None):
        if inc is not None:
            self.cnt[inc[0]] = self.cnt.get(inc[0], 0) + inc[1]
        self.ops[eng].append((tuple(w for w in waits if w and w[1] > 0), inc, fn))
        return self.cnt[inc[0]] if inc is not None else None

    def dma_pcf(self, q, dst, src, c, p=128, waits=(), sem="d0"):
        return self.dma(q, dst.rearrange("q (c f) -> q c f", c=c),
                        src.rearrange("(c q) f -> q c f", q=p),
                        waits=waits, sem=sem)

    def dma(self, q, out, in_, waits=(), slow=False, sem="d0"):
        def fn(e, out=out, in_=in_, slow=slow):
            if slow:
                return e.dma_start(out=out, in_=in_, allow_slow_non_contiguous=True)
            return e.dma_start(out=out, in_=in_)
        return self.emit(q, fn, waits=waits, inc=(sem, 16))

    def terminal(self):
        waits = [(k, v) for k, v in self.cnt.items()]
        self.ops["sync"].append((tuple(w for w in waits if w[1] > 0), None,
                                 lambda e: e.nop()))

    def replay(self, block, sems):
        for eng_name in ENGINES:
            oplist = self.ops[eng_name]

            def body(e, oplist=oplist):
                for waits, inc, fn in oplist:
                    for (s, v) in waits:
                        e.wait_ge(sems[s], v)
                    ins = fn(e)
                    if inc is not None:
                        ins.then_inc(sems[inc[0]], inc[1])
            getattr(block, eng_name)(body)


def _constants():
    c = {}
    c["ident"] = np.eye(128, dtype=np.float32)
    d_ = np.arange(D)
    ang_d = 2.0 * np.pi * np.outer(d_, d_) / D
    c["cd_in"] = np.cos(ang_d).astype(BF)
    c["sd_in"] = np.sin(ang_d).astype(BF)
    beta = np.arange(128)
    kap = np.arange(128)
    GRe = np.empty((64, 128, 128), np.float32)
    GIm = np.empty((64, 128, 128), np.float32)
    for a in range(64):
        ang = 2.0 * np.pi * (np.outer(beta, kap) / 128.0 + a * kap[None, :] / 8192.0)
        GRe[a] = np.cos(ang)
        GIm[a] = -np.sin(ang)
    c["gre_in"] = GRe.reshape(64 * 128, 128).astype(BF)
    c["gim_in"] = GIm.reshape(64 * 128, 128).astype(BF)
    c["gren_in"] = (-GRe).reshape(64 * 128, 128).astype(BF)
    return c


def _core_masks(s):
    dm = np.arange(128)[:, None]
    dt = np.arange(512)[None, :]
    D0 = (dt >= 4 * dm - 1).astype(np.float32)
    D1 = ((dm == 0) & (dt == 511)).astype(np.float32)
    ONES = np.ones((128, 512), np.float32)
    ZERO = np.zeros((128, 512), np.float32)
    tiles = []
    for p in range(NSEG):
        c_t = 4 * s + p
        for cm in range(14 + p):
            tiles.append(ONES if cm < c_t else
                         D0 if cm == c_t else
                         D1 if cm == c_t + 1 else ZERO)
    return np.concatenate(tiles, axis=0).astype(BF)


def _core_f64(s):
    lam = 16 * s + np.arange(16)
    alpha = np.arange(64)
    ang = 2.0 * np.pi * np.outer(alpha, lam) / 64.0
    return np.cos(ang).astype(BF), np.sin(ang).astype(BF)


def build_nc():
    nc = bass.Bass("TRN2", target_bir_lowering=False)
    dram = {}

    def din(name, shape, dtype=F32):
        dram[name] = nc.dram_tensor(name, list(shape), dtype, kind="ExternalInput")

    for name, shape, dt in [
        ("x_sl", (TL, D), F32), ("wqkv", (D, 3 * D), F32),
        ("conv_tw", (16, 4, 64, 64), F32), ("conv_b", (D,), F32),
        ("conv_bb", (128, D), F32), ("null_k", (D,), F32), ("null_v", (D,), F32),
        ("w_out", (D, D), F32), ("b_out", (D,), F32), ("gate_wt", (D, NE), F32),
        ("w1_e", (NE, D, FF), F32), ("w3_e", (NE, D, FF), F32),
        ("w2_e", (NE, FF, D), F32), ("w1_s", (NS, D, FS), F32),
        ("w3_s", (NS, D, FS), F32), ("w2_s", (NS, FS, D), F32),
        ("ident", (128, 128), F32), ("cd_in", (D, D), BF16),
        ("sd_in", (D, D), BF16), ("gre_in", (64 * 128, 128), BF16),
        ("gim_in", (64 * 128, 128), BF16), ("gren_in", (64 * 128, 128), BF16),
        ("f64c_in", (64, 16), BF16), ("f64s_in", (64, 16), BF16),
        ("masks_in", (62 * 128, 512), BF16),
    ]:
        din(name, shape, dt)

    dram["y_out"] = nc.dram_tensor("y_out", [TL, D], F32, kind="ExternalOutput")
    taps = {}
    tap_shapes = {
        "xnT": ((128, 8 * TL), BF16), "gates": ((128, NT * NE), F32),
        "ffn": ((128, NT * D), BF16), "outT": ((128, 8 * TL), BF16),
        "yT": ((128, 8 * TL), BF16), "kcT": ((64, 16 * 512), BF16),
        "vcL": ((128, 4 * 1024), BF16), "ab": ((4096, 1024), BF16),
        "xscr": ((TL, D), F32),
    }
    for t in DEBUG_TAPS:
        shp, dt = tap_shapes[t]
        taps[t] = nc.dram_tensor("tap_" + t, list(shp), dt, kind="ExternalOutput")

    for n, shape, dt in [
        ("qt_dram", (D, TL), BF16), ("cc1_in", (1032, 1024), BF16),
        ("cc1_out", (4 * 1032, 1024), BF16), ("cc2_in", (4096, 1024), BF16),
        ("cc2_out", (4 * 4096, 1024), BF16), ("y_dram", (2 * 64 * 128, 1024), BF16),
        ("x_scr", (TL, D), F32),
    ]:
        dram[n] = nc.dram_tensor(n, list(shape), dt)

    S = Sched()
    stack = contextlib.ExitStack()

    def sbuf(name, p, f, dt=BF16):
        return stack.enter_context(nc.sbuf_tensor(name, [p, f], dt))

    sb = {
        "arA": sbuf("arA", 128, 35088),
        "arB": sbuf("arB", 128, 16384),
        "arC": sbuf("arC", 128, 16384),
        "arD": sbuf("arD", 128, 20480),
        "arE": sbuf("arE", 128, 6144, F32),
        "cst": sbuf("cst", 128, 256, F32),
    }

    with (
        nc.psum_tensor("P_A", [128, 2048], F32) as P_A,
        nc.psum_tensor("P_B", [128, 2048], F32) as P_B,
    ):
        sb["P_A"], sb["P_B"] = P_A, P_B
        _emit(S, dram, taps, sb)
        names = sorted(set(S.cnt.keys())
                       | {w[0] for ops in S.ops.values() for (ws, _, _) in ops
                          for w in ws})
        sems = {n: stack.enter_context(nc.semaphore(n)) for n in names}
        with nc.Block() as block:
            S.replay(block, sems)

    stack.close()
    return nc


def _emit(S, dram, taps, sb):
    P_A, P_B = sb["P_A"], sb["P_B"]
    arA, arB, arC, arD, arE, cst = (sb[k] for k in
                                    ("arA", "arB", "arC", "arD", "arE", "cst"))
    dg = lambda n: dram[n].ap()

    xnT = arB
    outT = arB
    ffn = arC
    chunkb = [arD[:, i * 2048:(i + 1) * 2048] for i in range(3)]
    vc_l = arD[:, 6144:6144 + 4160]
    vc_l4 = vc_l.rearrange("p (jc h w) -> p jc h w", jc=4, h=H)
    wq_b = [arD[:, 14336 + i * 1024:14336 + (i + 1) * 1024] for i in range(2)]
    cbb_bf = arD[:, 16384:17408]
    ident_bf = arD[:, 17408:17536]
    gwt_bf = arD[:, 17536:17600]
    xns = [arD[:, 17664 + i * 1024:17664 + (i + 1) * 1024] for i in range(2)]
    convw_sb = arA[:, 0:4096]
    kcT_l = arA[0:64, 4096:4096 + 8192]
    w13_b = [(arD[:, i * 2048:i * 2048 + 1024], arD[:, i * 2048 + 1024:(i + 1) * 2048])
             for i in range(2)]
    w2half = arD[:, 4096:4096 + 16384]
    gu = arA[:, 0:32768]
    kcF = arA[:, 0:8 * M]
    vcF = arA[:, 8 * M:8 * M + NMC * 1040]
    vcF4 = vcF.rearrange("p (c h w) -> p c h w", c=NMC, h=H)
    qTs = arD[:, 0:4096]
    mskb = arD[:, 4096:4096 + NMC * 512]
    etile = [arD[:, 12800 + i * 512:12800 + (i + 1) * 512] for i in range(8)]
    cd_sb = arA[:, 0:8192]
    sd_sb = arA[:, 8192:16384]
    yT = arA[:, 16384:32768]
    wout_c = [arD[:, 16896 + i * 1024:16896 + (i + 1) * 1024] for i in range(2)]
    abA_t = [arD[:, 8192 + i * 1024:8192 + (i + 1) * 1024] for i in range(2)]
    abB_t = [arD[:, 10240 + i * 1024:10240 + (i + 1) * 1024] for i in range(2)]
    gre_sb = arA[:, 0:8192]
    gim_sb = arA[:, 8192:16384]
    gren_sb = arA[:, 16384:24576]
    yev_t = [arA[:, 24576 + i * 2048:24576 + (i + 1) * 2048] for i in range(2)]
    zre_t = [arA[:, 28672 + i * 1024:28672 + (i + 1) * 1024] for i in range(2)]
    zim_t = [arA[:, 30720 + i * 1024:30720 + (i + 1) * 1024] for i in range(2)]
    s3re_t = [arA[0:64, 24576 + i * 4096:24576 + (i + 1) * 4096] for i in range(2)]
    s3im_t = [arA[0:64, i * 4096:(i + 1) * 4096] for i in range(2)]
    f64_sb = arA[0:64, 32768:32800]
    w2s_v = arD[:, 4096:8192]
    gus_v = arD[:, 12288:20480]
    xt_f = [arE[:, i * 1024:(i + 1) * 1024] for i in range(2)]
    sqscr = arE[:, 2048:3072]
    bc_sb = [arE[:, 2048 + i * 512:2048 + (i + 1) * 512] for i in range(4)]
    rs_r = [arE[0:1, 4096 + i * 512:4096 + (i + 1) * 512] for i in range(4)]
    gates = arE[:, 3072:3200]
    glog = arE[:, 3200:3264]
    stat = arE[:, 3264:3328]
    xf_t = [arE[:, 2048 + i * 1024:2048 + (i + 1) * 1024] for i in range(2)]
    ot_f = [arE[:, 4096 + i * 1024:4096 + (i + 1) * 1024] for i in range(2)]
    s3e_t = [arE[0:16, 4096 + i * 1024:4096 + (i + 1) * 1024] for i in range(2)]
    convb_c = cst[:, 0:8]
    bout_c = cst[:, 8:16]
    ones_f = cst[0:1, 16:80]
    ident_f = cst[:, 128:256]

    TT = {k: taps[k].ap() for k in taps}

    S.dma("gpsimd", cbb_bf, dg("conv_bb")[:, :], sem="gc")
    S.dma("sync", ident_f, dg("ident")[:, :], sem="cst")
    S.dma_pcf("gpsimd", gwt_bf, dg("gate_wt"), c=8, sem="gc")
    for bp0 in (0, 64):
        S.dma("gpsimd",
              convw_sb[bp0:bp0 + 64, :].rearrange("i (g t o) -> i g t o", g=16, t=4),
              dg("conv_tw").rearrange("g t i o -> i g t o"), sem="gc")
    m_cg = S.val("gc")
    S.dma("sync", convb_c, dg("conv_b").rearrange("(c p) -> p c", p=128), slow=True,
          sem="cst")
    S.dma("sync", bout_c, dg("b_out").rearrange("(c p) -> p c", p=128), slow=True,
          sem="cst")
    m_cs = S.val("cst")
    m_ones_f = S.emit("gpsimd", lambda e: e.memset(ones_f, 1.0), inc=("gS", 1))

    x_ap = dg("x_sl")
    m_ld = [0] * NT; m_sq = [0] * NT; m_var = [0] * NT; m_rstd = [0] * NT
    m_nb = [0] * NT; m_xn = [0] * NT; m_tr = [0] * NT; m_tev = [0] * NT
    inv_d = 1.0 / D
    for i in range(NT):
        buf = xt_f[i % 2]
        w = [("tS", m_tr[i - 2])] if i >= 2 else []
        m_ld[i] = S.dma("sync", buf, x_ap[i * 128:(i + 1) * 128, :], waits=w,
                        sem="x%d" % (i % 2))
        c0 = stat[:, (i % 4) * 8:(i % 4) * 8 + 8]
        s1, s2, mean, ex2, var, rstd, nb, tmp = (c0[:, j:j + 1] for j in range(8))
        mv = S.emit("vector",
                    lambda e, b=buf, o=s1: e.tensor_reduce(o, b, axis=AX.X, op=ALU.add),
                    waits=[("x%d" % (i % 2), m_ld[i])], inc=("vS", 1))
        m_sq[i] = S.emit("scalar",
                         lambda e, b=buf, o=s2: e.activation(sqscr, b, AF.Square, accum_out=o),
                         waits=[("x%d" % (i % 2), m_ld[i])], inc=("sS", 1))
        mv = S.emit("vector",
                    lambda e, a=mean, b=s1: e.tensor_scalar(a, b, inv_d, None, op0=ALU.mult),
                    waits=[("sS", m_sq[i]), ("vS", mv)], inc=("vS", 1))
        mv = S.emit("vector", lambda e, a=ex2, b=s2:
                    e.tensor_scalar(a, b, inv_d, 1e-5, op0=ALU.mult, op1=ALU.add),
                    waits=[("vS", mv)], inc=("vS", 1))
        mv = S.emit("vector", lambda e, a=tmp, b=mean: e.tensor_tensor(a, b, b, op=ALU.mult),
                    waits=[("vS", mv)], inc=("vS", 1))
        m_var[i] = S.emit("vector",
                          lambda e, a=var, b=ex2, c=tmp: e.tensor_tensor(a, b, c, op=ALU.subtract),
                          waits=[("vS", mv)], inc=("vS", 1))
        ms = S.emit("scalar", lambda e, a=tmp, b=var: e.activation(a, b, AF.Ln),
                    waits=[("vS", m_var[i])], inc=("sS", 1))
        m_rstd[i] = S.emit("scalar",
                           lambda e, a=rstd, b=tmp: e.activation(a, b, AF.Exp, scale=-0.5),
                           waits=[("sS", ms)], inc=("sS", 1))
        mv = S.emit("vector", lambda e, a=tmp, b=mean, c=rstd: e.tensor_tensor(a, b, c, op=ALU.mult),
                    waits=[("sS", m_rstd[i]), ("vS", m_var[i])], inc=("vS", 1))
        m_nb[i] = S.emit("vector",
                         lambda e, a=nb, b=tmp: e.tensor_scalar(a, b, -1.0, None, op0=ALU.mult),
                         waits=[("vS", mv)], inc=("vS", 1))
        wx = [("vS", m_nb[i])]
        m_xn[i] = S.emit("scalar",
                         lambda e, b=buf, scc=rstd, bi=nb:
                         e.activation(b, b, AF.Identity, bias=bi, scale=scc),
                         waits=wx, inc=("sS", 1))
        for j in range(8):
            w2 = []
            if j == 0:
                w2 = [("sS", m_xn[i]), ("cst", m_cs)]
                if i >= 1:
                    w2.append(("vS", m_tev[i - 1]))
            m_tr[i] = S.emit("tensor",
                             lambda e, j=j, src=buf:
                             e.transpose(P_B[:, j * 128:(j + 1) * 128],
                                         src[:, j * 128:(j + 1) * 128], ident_f),
                             waits=w2, inc=("tS", 1) if j == 7 else None)
        def evac_fn(e, i=i):
            out_ap = xnT.rearrange("p (c t) -> p c t", c=8)[:, :, i * 128:(i + 1) * 128]
            return e.tensor_copy(out_ap,
                                 P_B[:, 0:1024].rearrange("p (c t) -> p c t", c=8))
        m_tev[i] = S.emit("vector", evac_fn, waits=[("tS", m_tr[i])], inc=("vS", 1))
    m_xnT = m_tev[NT - 1]
    if "xnT" in TT:
        S.dma("sync", TT["xnT"], xnT[:, :], waits=[("vS", m_xnT)], sem="tap")

    if PHASE_LIM < 1:
        S.terminal()
        return
    m_wq = [0] * 24
    m_qseg = [[0] * NSEG for _ in range(24)]
    m_qev = [[0] * NSEG for _ in range(24)]
    m_park = [0] * 8
    m_convk = [0] * 24
    m_kcev = {-2: 0, -1: 0}
    m_vcev = {}
    wqkv_ap = dg("wqkv")
    for fc in range(2):
        m_wq[fc] = S.dma_pcf("gpsimd", wq_b[fc % 2],
                             wqkv_ap[:, fc * 128:(fc + 1) * 128], c=8,
                             sem="wq%d" % (fc % 2))
    for fc in range(24):
        cb = chunkb[fc % 3]
        pw = []
        if fc >= 3:
            pfc = fc - 3
            pw = [("pk%d" % (pfc % 3), m_park[pfc])] if pfc < 8 else [("tS", m_convk[pfc])]
        for seg in range(NSEG):
            for dc in range(8):
                wts = []
                if dc == 0:
                    wts = [("wq%d" % (fc % 2), m_wq[fc]), ("vS", m_xnT)]
                    if fc >= 1:
                        wts.append(("vS", m_qev[fc - 1][seg]))
                    if seg == 0:
                        wts += pw
                m_qseg[fc][seg] = S.emit(
                    "tensor",
                    lambda e, seg=seg, dc=dc, fc=fc:
                    e.matmul(P_A[:, seg * 512:(seg + 1) * 512],
                             wq_b[fc % 2][:, dc * 128:(dc + 1) * 128],
                             xnT[:, dc * 2048 + seg * 512:dc * 2048 + (seg + 1) * 512],
                             start=(dc == 0), stop=(dc == 7)),
                    waits=wts, inc=("tS", 1) if dc == 7 else None)
            m_qev[fc][seg] = S.emit(
                "vector",
                lambda e, cb=cb, seg=seg:
                e.tensor_copy(cb[:, seg * 512:(seg + 1) * 512],
                              P_A[:, seg * 512:(seg + 1) * 512]),
                waits=[("tS", m_qseg[fc][seg])], inc=("vS", 1))
        if fc + 2 < 24:
            m_wq[fc + 2] = S.dma_pcf(
                "gpsimd", wq_b[fc % 2], wqkv_ap[:, (fc + 2) * 128:(fc + 3) * 128],
                c=8, waits=[("tS", m_qseg[fc][NSEG - 1])], sem="wq%d" % (fc % 2))
        if fc < 8:
            m_park[fc] = S.dma("sync", dg("qt_dram")[fc * 128:(fc + 1) * 128, :], cb[:, :],
                               waits=[("vS", m_qev[fc][NSEG - 1])],
                               sem="pk%d" % (fc % 3))
        elif fc < 16:
            cc = fc - 8
            for gl in range(2):
                g = 2 * cc + gl
                bp = (g % 2) * 64
                pbv = P_B[0:64, (g % 2) * 512:(g % 2) * 512 + 512]
                for tau in range(CF):
                    wts = []
                    if tau == 0:
                        wts = [("vS", m_qev[fc][NSEG - 1]), ("sS", m_kcev[g - 2])]
                        if g < 2:
                            wts += [("vS", m_tev[NT - 1]), ("gc", m_cg)]
                    m_convk[fc] = S.emit(
                        "tensor",
                        lambda e, g=g, tau=tau, cb=cb, pbv=pbv, bp=bp:
                        e.matmul(pbv,
                                 convw_sb[bp:bp + 64, (g * 4 + tau) * 64:(g * 4 + tau + 1) * 64],
                                 cb[bp:bp + 64, tau:2048:4],
                                 start=(tau == 0), stop=(tau == 3)),
                        waits=wts, inc=("tS", 1) if tau == 3 else None)
                m_kcev[g] = S.emit(
                    "scalar",
                    lambda e, g=g, pbv=pbv, bp=bp:
                    e.activation(kcT_l[:, g * 512:(g + 1) * 512], pbv, AF.Identity,
                                 bias=convb_c[bp:bp + 64, (g // 2):(g // 2) + 1]),
                    waits=[("tS", m_convk[fc]), ("cst", m_cs)], inc=("sS", 1))
        else:
            cc = fc - 16
            for gl in range(2):
                g = 2 * cc + gl
                bp = (g % 2) * 64
                pbv = P_B[:, (g % 2) * 512:(g % 2) * 512 + 64]
                for jc in range(4):
                    for tau in range(CF):
                        wts = []
                        if tau == 0:
                            wts = [("vS", m_vcev.get((g, jc - 1),
                                                     m_vcev.get((g - 2, 3), 0)))]
                            if jc == 0:
                                wts.append(("vS", m_qev[fc][NSEG - 1]))
                                if g < 2:
                                    wts += [("sS", m_kcev[15]), ("gc", m_cg)]
                        m_convk[fc] = S.emit(
                            "tensor",
                            lambda e, g=g, tau=tau, jc=jc, cb=cb, pbv=pbv, bp=bp:
                            e.matmul(pbv,
                                     cb[bp:bp + 64, jc * 512 + tau:jc * 512 + tau + 509:4],
                                     convw_sb[bp:bp + 64,
                                              (g * 4 + tau) * 64:(g * 4 + tau + 1) * 64],
                                     start=(tau == 0), stop=(tau == 3)),
                            waits=wts, inc=("tS", 1) if tau == 3 else None)
                    m_vcev[(g, jc)] = S.emit(
                        "vector",
                        lambda e, g=g, jc=jc, pbv=pbv:
                        e.tensor_copy(vc_l4[:, jc, g, 0:64], pbv),
                        waits=[("tS", m_convk[fc])], inc=("vS", 1))
    m_vcb = 0
    for jc in range(4):
        m_vcb = S.emit("vector",
                       lambda e, jc=jc:
                       e.tensor_tensor(vc_l4[:, jc, :, 0:64],
                                       vc_l4[:, jc, :, 0:64],
                                       cbb_bf.rearrange("p (h w) -> p h w", h=H),
                                       op=ALU.add),
                       waits=[("vS", m_vcev[(15, jc)]), ("gc", m_cg)], inc=("vS", 1))
        m_vcb = S.emit("vector",
                       lambda e, jc=jc: e.memset(vc_l4[:, jc, :, 64:65], 1.0),
                       waits=[("vS", m_vcb)], inc=("vS", 1))
    if "kcT" in TT:
        S.dma("sync", TT["kcT"], kcT_l[:, :], waits=[("sS", m_kcev[15])], sem="tap")
    if "vcL" in TT:
        S.dma("sync", TT["vcL"], vc_l[:, :], waits=[("vS", m_vcb)], sem="tap")

    kc_dst = dg("cc1_in")[0:512, :].rearrange("x d -> (x d)") \
        .rearrange("(g o j) -> o g j", o=64, g=16)
    m_kcb = S.dma("sync", kc_dst,
                  kcT_l.rearrange("o (g j) -> o g j", g=16),
                  waits=[("sS", m_kcev[15])], sem="bnc")
    m_b2 = S.dma("sync",
                 dg("cc1_in")[512:1032, :].rearrange("x d -> (x d)")
                 .rearrange("(c q w) -> q c w", q=128, c=4),
                 vc_l.rearrange("q (c w) -> q c w", c=4), waits=[("vS", m_vcb)],
                 sem="bnc")

    def cc1_fn(e):
        return e.collective_compute(
            "AllGather", ALU.bypass,
            replica_groups=[[0, 1, 2, 3], [4, 5, 6, 7]],
            ins=[dg("cc1_in")], outs=[dg("cc1_out")])
    S.emit("gpsimd", cc1_fn, waits=[("bnc", m_b2)], inc=("c1", 1))

    if PHASE_LIM < 2:
        S.terminal()
        return
    m_gl = [0] * NT
    m_gev = [0] * NT
    for tt in range(NT):
        for dc in range(8):
            wts = []
            if dc == 0:
                wts = [("gc", m_cg), ("vS", m_xnT)]
                if tt >= 2:
                    wts.append(("vS", m_gev[tt - 2]))
                if tt == 0:
                    wts += [("sS", m_kcev[15]), ("vS", m_vcb)]
            m_gl[tt] = S.emit(
                "tensor",
                lambda e, tt=tt, dc=dc:
                e.matmul(P_B[:, 512 + (tt % 2) * 512:512 + (tt % 2) * 512 + 8],
                         xnT[:, dc * 2048 + tt * 128:dc * 2048 + (tt + 1) * 128],
                         gwt_bf[:, dc * 8:(dc + 1) * 8],
                         start=(dc == 0), stop=(dc == 7)),
                waits=wts, inc=("tS", 1) if dc == 7 else None)
        c0 = glog[:, (tt % 2) * 32:(tt % 2) * 32 + 32]
        lg = c0[:, 0:8]; t2 = c0[:, 8:16]
        l1 = c0[:, 16:17]; nl1 = c0[:, 17:18]; l2 = c0[:, 18:19]
        sm = c0[:, 19:20]; rc = c0[:, 20:21]; ex = c0[:, 24:32]
        t3 = c0[:, 24:32]
        nk = stat[:, (tt % 2) * 8:(tt % 2) * 8 + 8]
        pbl = P_B[:, 512 + (tt % 2) * 512:512 + (tt % 2) * 512 + 8]
        mv = S.emit("vector", lambda e, lg=lg, pbl=pbl: e.tensor_copy(lg, pbl),
                    waits=[("tS", m_gl[tt])], inc=("vS", 1))
        mv = S.emit("vector",
                    lambda e, l1=l1, lg=lg: e.tensor_reduce(l1, lg, axis=AX.X, op=ALU.max),
                    waits=[("vS", mv)], inc=("vS", 1))
        mv = S.emit("vector", lambda e, t2=t2, lg=lg, l1=l1:
                    e.tensor_scalar(t2, lg, l1, None, op0=ALU.is_lt),
                    waits=[("vS", mv)], inc=("vS", 1))
        mv = S.emit("vector", lambda e, nk=nk, t2=t2:
                    e.tensor_scalar(nk, t2, 1e9, -1e9, op0=ALU.mult, op1=ALU.add),
                    waits=[("vS", mv)], inc=("vS", 1))
        mv = S.emit("vector", lambda e, t3=t3, t2=t2, lg=lg:
                    e.tensor_tensor(t3, lg, t2, op=ALU.mult),
                    waits=[("vS", mv)], inc=("vS", 1))
        mv = S.emit("vector", lambda e, t3=t3, nk=nk:
                    e.tensor_tensor(t3, t3, nk, op=ALU.add),
                    waits=[("vS", mv)], inc=("vS", 1))
        mv = S.emit("vector",
                    lambda e, l2=l2, t3=t3: e.tensor_reduce(l2, t3, axis=AX.X, op=ALU.max),
                    waits=[("vS", mv)], inc=("vS", 1))
        m_v1 = S.emit("vector",
                      lambda e, nl1=nl1, l1=l1: e.tensor_scalar(nl1, l1, -1.0, None, op0=ALU.mult),
                      waits=[("vS", mv)], inc=("vS", 1))
        m_s1 = S.emit("scalar",
                      lambda e, ex=ex, lg=lg, nl1=nl1: e.activation(ex, lg, AF.Exp, bias=nl1),
                      waits=[("vS", m_v1)], inc=("sS", 1))
        mv = S.emit("vector",
                    lambda e, sm=sm, ex=ex: e.tensor_reduce(sm, ex, axis=AX.X, op=ALU.add),
                    waits=[("sS", m_s1), ("vS", m_v1)], inc=("vS", 1))
        mv = S.emit("vector", lambda e, rc=rc, sm=sm: e.reciprocal(rc, sm),
                    waits=[("vS", mv)], inc=("vS", 1))
        mv = S.emit("vector", lambda e, rc=rc: e.tensor_scalar(rc, rc, 0.5, None, op0=ALU.mult),
                    waits=[("vS", mv)], inc=("vS", 1))
        mv = S.emit("vector", lambda e, t2=t2, lg=lg, l2=l2:
                    e.tensor_scalar(t2, lg, l2, None, op0=ALU.is_ge),
                    waits=[("vS", mv)], inc=("vS", 1))
        mv = S.emit("vector", lambda e, t2=t2, ex=ex: e.tensor_tensor(t2, t2, ex, op=ALU.mult),
                    waits=[("vS", mv)], inc=("vS", 1))
        m_gev[tt] = S.emit("vector",
                           lambda e, tt=tt, t2=t2, rc=rc:
                           e.tensor_scalar(gates[:, tt * 8:(tt + 1) * 8], t2, rc, None,
                                           op0=ALU.mult),
                           waits=[("vS", mv)], inc=("vS", 1))
    m_gates = m_gev[NT - 1]
    if "gates" in TT:
        S.dma("sync", TT["gates"], gates, waits=[("vS", m_gates)], sem="tap")

    if PHASE_LIM < 3:
        S.terminal()
        return
    m_ffn0 = S.emit("vector", lambda e: e.memset(ffn[:, :], 0.0), inc=("vS", 1))

    state = {"last_phB": m_gl[NT - 1], "ev_idx": 0, "phA_q": 0,
             "moe_prev_silu": [0, 0], "moe_prev_gu": [0, 0],
             "last_phA": 0, "prev_w13_consumed": [m_gl[NT - 1], m_gl[NT - 1]],
             "w2_extra": [("bnc", m_b2), ("vS", m_vcb)],
             "w13_extra": [],
             "silu_extra": [("bnc", m_b2)]}
    m_ffn_ev = {-3: 0, -2: 0, -1: 0}

    def moe_pass(f_hid, w1_ap, w3_ap, w2_ap, gate_col, gate_const,
                 gu_v=None, w2_v=None, w13_v=None):
        gu_l = gu_v if gu_v is not None else gu
        w2_l = w2_v if w2_v is not None else w2half
        w13_l = w13_v if w13_v is not None else w13_b
        nfc = f_hid // 128
        PS = (P_A, P_B)
        mb01 = [0, 0]
        for fc0 in range(min(2, nfc)):
            S.dma_pcf("gpsimd", w13_l[fc0][0],
                      w1_ap[:, fc0 * 128:(fc0 + 1) * 128], c=8,
                      waits=[("tS", state["prev_w13_consumed"][fc0])]
                      + state["w13_extra"],
                      sem="wq%d" % fc0)
            state["w13_extra"] = []
            mb01[fc0] = S.dma_pcf("gpsimd", w13_l[fc0][1],
                                  w1_ap if False else w3_ap[:, fc0 * 128:(fc0 + 1) * 128],
                                  c=8, sem="wq%d" % fc0)
        m_w2 = S.dma_pcf("gpsimd", w2_l[:, 0:nfc * 1024], w2_ap, c=nfc,
                         waits=[("tS", state["last_phB"])] + state["w2_extra"],
                         sem="w2s")
        state["w2_extra"] = []
        m_gu_last = 0
        m_phA = {}
        mbv = dict(enumerate(mb01))
        prev_silu = state["moe_prev_silu"]
        prev_gu = state["moe_prev_gu"]
        pg = m_ffn_ev.get(state["ev_idx"] - 1, 0)
        prev_gu[0] = max(prev_gu[0], pg)
        prev_gu[1] = max(prev_gu[1], pg)
        for fc in range(nfc):
            mb = mbv[fc]
            m_w1 = 0
            for dc in range(8):
                for sg in range(4):
                    wts = []
                    if dc == 0 and sg == 0:
                        wts = [("wq%d" % (fc % 2), mb), ("vS", m_gates),
                               ("sS", prev_silu[0]), ("vS", prev_gu[0])]
                    m_w1 = S.emit(
                        "tensor",
                        lambda e, sg=sg, dc=dc, fc=fc:
                        e.matmul(P_A[:, sg * 512:(sg + 1) * 512],
                                 w13_l[fc % 2][0][:, dc * 128:(dc + 1) * 128],
                                 xnT[:, dc * 2048 + sg * 512:dc * 2048 + (sg + 1) * 512],
                                 start=(dc == 0), stop=(dc == 7)),
                        waits=wts, inc=("tS", 1) if (sg == 3 and dc == 7) else None)
            prev_silu[0] = S.emit(
                "scalar",
                lambda e, fc=fc:
                e.activation(gu_l[:, fc * 2048:(fc + 1) * 2048], P_A[:, :], AF.Silu),
                waits=[("tS", m_w1)] + state["silu_extra"], inc=("sS", 1))
            state["silu_extra"] = []
            mlast = 0
            for dc in range(8):
                for sg in range(4):
                    wts = []
                    if dc == 0 and sg == 0:
                        wts = [("vS", prev_gu[1])]
                    mlast = S.emit(
                        "tensor",
                        lambda e, sg=sg, dc=dc, fc=fc:
                        e.matmul(P_B[:, sg * 512:(sg + 1) * 512],
                                 w13_l[fc % 2][1][:, dc * 128:(dc + 1) * 128],
                                 xnT[:, dc * 2048 + sg * 512:dc * 2048 + (sg + 1) * 512],
                                 start=(dc == 0), stop=(dc == 7)),
                        waits=wts, inc=("tS", 1) if (sg == 3 and dc == 7) else None)
            m_phA[fc] = mlast
            state["last_phA"] = mlast
            prev_gu[1] = S.emit(
                "vector",
                lambda e, fc=fc:
                e.tensor_tensor(gu_l[:, fc * 2048:(fc + 1) * 2048],
                                gu_l[:, fc * 2048:(fc + 1) * 2048],
                                P_B[:, :], op=ALU.mult),
                waits=[("sS", prev_silu[0]), ("tS", mlast)], inc=("vS", 1))
            state["prev_w13_consumed"][fc % 2] = m_phA[fc]
            if fc + 2 < nfc:
                S.dma_pcf("gpsimd", w13_l[fc % 2][0],
                          w1_ap[:, (fc + 2) * 128:(fc + 3) * 128], c=8,
                          waits=[("tS", m_phA[fc])], sem="wq%d" % (fc % 2))
                mbv[fc + 2] = S.dma_pcf("gpsimd", w13_l[fc % 2][1],
                                        w3_ap[:, (fc + 2) * 128:(fc + 3) * 128],
                                        c=8, sem="wq%d" % (fc % 2))
            m_gu_last = prev_gu[1]
        m_gu_all = max(prev_gu[0], prev_gu[1])
        for tt in range(NT):
            ev = state["ev_idx"]
            bs = PS[tt % 2]
            mlast = 0
            for fc in range(nfc):
                for do in range(2):
                    wts = []
                    if fc == 0 and do == 0:
                        wts = [("vS", m_gu_all), ("w2s", m_w2),
                               ("vS", m_ffn_ev[ev - 3]),
                               ("sS", prev_silu[0])]
                    mlast = S.emit(
                        "tensor",
                        lambda e, tt=tt, do=do, fc=fc, bs=bs, nfc=nfc:
                        e.matmul(bs[:, do * 512:(do + 1) * 512],
                                 gu_l[:, fc * 2048 + tt * 128:fc * 2048 + (tt + 1) * 128],
                                 w2_l[:, fc * 1024 + do * 512:fc * 1024 + (do + 1) * 512],
                                 start=(fc == 0), stop=(fc == nfc - 1)),
                        waits=wts, inc=("tS", 1) if fc == nfc - 1 else None)
            state["last_phB"] = mlast
            gc = gate_col(tt) if gate_col is not None else gate_const
            for do in range(2):
                m_ffn_ev[state["ev_idx"]] = S.emit(
                    "vector",
                    lambda e, tt=tt, do=do, bs=bs, gc=gc:
                    e.scalar_tensor_tensor(
                        ffn[:, tt * 1024 + do * 512:tt * 1024 + (do + 1) * 512],
                        bs[:, do * 512:(do + 1) * 512], gc,
                        ffn[:, tt * 1024 + do * 512:tt * 1024 + (do + 1) * 512],
                        op0=ALU.mult, op1=ALU.add),
                    waits=[("tS", mlast), ("vS", m_ffn0)], inc=("vS", 1))
                state["ev_idx"] += 1

    for e_i in range(NE):
        for hh in range(2):
            moe_pass(2048,
                     dg("w1_e")[e_i][:, hh * 2048:(hh + 1) * 2048],
                     dg("w3_e")[e_i][:, hh * 2048:(hh + 1) * 2048],
                     dg("w2_e")[e_i][hh * 2048:(hh + 1) * 2048, :],
                     lambda tt, e_i=e_i: gates[:, tt * 8 + e_i:tt * 8 + e_i + 1], None)
    m_moe_t = state["last_phB"]
    m_moe_tA = state["last_phA"]
    if "ffn" in TT:
        S.dma("sync", TT["ffn"], ffn[:, :], waits=[("vS", m_moe_v)], sem="tap")

    if PHASE_LIM < 4:
        S.terminal()
        return
    kcF4 = kcF.rearrange("p (c m) -> p c m", c=8)
    m_z1 = S.emit("vector", lambda e: e.memset(kcF4[:, :, 2049:M], 0.0),
                  waits=[("tS", m_moe_t), ("vS", m_ffn_ev[state["ev_idx"] - 1])],
                  inc=("vS", 1))
    m_z2 = S.emit("vector", lambda e: e.memset(vcF4[:, NMC - 1, :, :], 0.0),
                  waits=[("vS", m_z1)], inc=("vS", 1))
    m_onesv = S.emit("vector",
                     lambda e: e.memset(vcF4[0:1, 0, :, 64:65], 1.0),
                     waits=[("vS", m_z2)], inc=("vS", 1))
    ccv = dg("cc1_out").rearrange("(r x) d -> r (x d)", r=4)
    vcg_r = [ccv[r][512 * 1024:1032 * 1024].rearrange("(x w) -> x w", x=512)
             for r in range(4)]
    m_kcl = 0
    kc_w_extra = {"sync": True, "scalar": True}
    for r in range(4):
        kcr = ccv[r][0:512 * 1024].rearrange("(f j) -> f j", f=1024)
        for dc in range(8):
            q = "scalar" if dc % 2 else "sync"
            wts = []
            if kc_w_extra[q]:
                kc_w_extra[q] = False
                wts = [("c1", 1), ("tS", m_moe_t),
                       ("vS", m_ffn_ev[state["ev_idx"] - 1])]
            m_kcl = S.dma(q,
                          kcF[:, dc * M + 1 + r * 512:dc * M + 1 + (r + 1) * 512],
                          kcr[dc * 128:(dc + 1) * 128, :], waits=wts, sem="kcl")
    m_vcl = 0
    for cm in range(NMC):
        lo = max(cm * 128 - 1, 0)
        hi = min(cm * 128 + 127, 2048)
        dp0 = 1 if cm == 0 else 0
        wts = [("c1", 1), ("tS", m_moe_t),
               ("vS", m_ffn_ev[state["ev_idx"] - 1])] if cm == 0 else \
            ([("vS", m_onesv)] if cm == NMC - 1 else [])
        j = lo
        while j < hi:
            r = j // 512
            je = min(hi, (r + 1) * 512)
            n = je - j
            dp = dp0 + (j - lo)
            m_vcl = S.dma("sync", vcF4[dp:dp + n, cm, :, :],
                          vcg_r[r][j - r * 512:je - r * 512, :]
                          .rearrange("x (h w) -> x h w", h=H, w=65),
                          waits=wts, sem="vcl")
            wts = []
            j = je
    m_nk = 0
    for dc in range(8):
        wts = [("tS", m_moe_t), ("vS", m_ffn_ev[state["ev_idx"] - 1])] \
            if dc == 0 else []
        m_nk = S.dma("gpsimd", kcF[:, dc * M:dc * M + 1],
                     dg("null_k")[dc * 128:(dc + 1) * 128].rearrange("(p o) -> p o", o=1),
                     waits=wts, slow=True, sem="nul")
    m_nv = S.dma("gpsimd", vcF4[0:1, 0, :, 0:64],
                 dg("null_v").rearrange("(o h w) -> o h w", o=1, h=H),
                 waits=[("vS", m_onesv)], sem="nul")

    m_qp = [0] * NSEG
    m_mp = [0] * NSEG
    m_oev = {}
    last_exp = 0
    exp_slot = [0, 0, 0, 0]
    last_av = {i: 0 for i in range(8)}
    last_msk = 0
    last_sc_mm = 0
    mv_prev = 0
    oev_prev = 0
    for p in range(NSEG):
        n_p = 14 + p
        wq = [("tS", m_moe_tA)] if p == 0 else [("tS", last_sc_mm)]
        m_qp[p] = S.dma_pcf("sync", qTs,
                            dg("qt_dram")[:, p * 512:(p + 1) * 512], c=8, waits=wq,
                            sem="qts")
        wm = [("tS", m_moe_t)] if p == 0 else [("vS", last_msk)]
        m_mp[p] = S.dma_pcf("sync", mskb[:, 0:n_p * 512],
                            dg("masks_in")[MASK_OFF[p] * 128:(MASK_OFF[p] + n_p) * 128, :],
                            c=n_p, waits=wm, sem="msk")
        for hg in range(H // 4):
            heads = [4 * hg + i for i in range(4)]
            for cm in range(n_p):
                eb = [4 * (cm % 2) + i for i in range(4)]
                for i, h in enumerate(heads):
                    bp = (h % 2) * 64
                    wts = [("sS", exp_slot[i])]
                    if i == 0 and cm == 0:
                        if hg == 0:
                            wts += [("qts", m_qp[p]), ("kcl", m_kcl), ("nul", m_nv)]
                        wts.append(("vS", mv_prev))
                    last_sc_mm = S.emit(
                        "tensor",
                        lambda e, h=h, i=i, cm=cm, bp=bp:
                        e.matmul(P_A[:, i * 512:(i + 1) * 512],
                                 kcF[bp:bp + 64,
                                     (h // 2) * M + cm * 128:(h // 2) * M + (cm + 1) * 128],
                                 qTs[bp:bp + 64, (h // 2) * 512:(h // 2) * 512 + 512],
                                 start=True, stop=True),
                        waits=wts, inc=("tS", 1))
                ebase = arD[:, 12800 + 4 * (cm % 2) * 512:12800 + (4 * (cm % 2) + 4) * 512]
                for j in range(2):
                    last_exp = S.emit(
                        "scalar",
                        lambda e, j=j, ebase=ebase:
                        e.activation(ebase[:, j * 1024:(j + 1) * 1024],
                                     P_A[:, j * 1024:(j + 1) * 1024],
                                     AF.Exp, scale=float(D) ** -0.5),
                        waits=([("tS", last_sc_mm)] if j == 0 else [])
                        + [("tS", last_av[eb[2 * j]]), ("tS", last_av[eb[2 * j + 1]])],
                        inc=("sS", 1))
                    exp_slot[2 * j] = exp_slot[2 * j + 1] = last_exp
                for i in range(4):
                    last_msk = S.emit(
                        "vector",
                        lambda e, i=i, cm=cm, eb=eb:
                        e.tensor_tensor(etile[eb[i]], etile[eb[i]],
                                        mskb[:, cm * 512:(cm + 1) * 512], op=ALU.mult),
                        waits=([("sS", last_exp), ("msk", m_mp[p])] if i == 0 else []),
                        inc=("vS", 1))
                for i, h in enumerate(heads):
                    wav = []
                    if i == 0:
                        wav = [("vS", last_msk)]
                        if cm == 0:
                            wav += [("vcl", m_vcl), ("nul", m_nv), ("vS", oev_prev)]
                    last_av[eb[i]] = S.emit(
                        "tensor",
                        lambda e, h=h, i=i, cm=cm, n_p=n_p, eb=eb:
                        e.matmul(P_B[0:65, i * 512:(i + 1) * 512],
                                 vcF[:, cm * 1040 + h * 65:cm * 1040 + h * 65 + 65],
                                 etile[eb[i]],
                                 start=(cm == 0), stop=(cm == n_p - 1)),
                        waits=wav, inc=("tS", 1))
            m_av_all = last_av[eb[3]]
            m_rc = 0
            for i in range(4):
                m_rc = S.emit("vector",
                              lambda e, i=i:
                              e.reciprocal(rs_r[i][:, :], P_B[64:65, i * 512:(i + 1) * 512]),
                              waits=([("tS", m_av_all), ("sS", last_exp)] if i == 0 else []),
                              inc=("vS", 1))
            m_bc = 0
            for i in range(4):
                m_bc = S.emit("tensor",
                              lambda e, i=i:
                              e.matmul(P_A[0:64, i * 512:(i + 1) * 512], ones_f,
                                       rs_r[i][:, :], start=True, stop=True),
                              waits=([("vS", m_rc), ("gS", m_ones_f)] if i == 0 else []),
                              inc=("tS", 1))
            mv = 0
            for i in range(4):
                mv = S.emit("vector",
                            lambda e, i=i:
                            e.tensor_copy(bc_sb[i][0:64, :], P_A[0:64, i * 512:(i + 1) * 512]),
                            waits=([("tS", m_bc), ("vS", m_rc)] if i == 0 else []),
                            inc=("vS", 1))
            mv_prev = mv
            for i, h in enumerate(heads):
                oev_prev = m_oev[(p, hg)] = S.emit(
                    "vector",
                    lambda e, p=p, h=h, i=i:
                    e.tensor_tensor(outT[(h % 2) * 64:(h % 2) * 64 + 64,
                                         (h // 2) * 2048 + p * 512:(h // 2) * 2048 + (p + 1) * 512],
                                    bc_sb[i][0:64, :], P_B[0:64, i * 512:(i + 1) * 512],
                                    op=ALU.mult),
                    waits=([("vS", mv)] if i == 0 else []), inc=("vS", 1))
    m_attn_v = m_oev[(NSEG - 1, H // 4 - 1)]
    m_attn_t = S.val("tS")
    if "outT" in TT:
        S.dma("sync", TT["outT"], outT[:, :], waits=[("vS", m_attn_v)], sem="tap")

    if PHASE_LIM < 5:
        S.terminal()
        return
    m_wo = {}
    for fcc in range(2):
        m_wo[fcc] = S.dma_pcf("gpsimd", wout_c[fcc],
                              dg("w_out")[:, fcc * 128:(fcc + 1) * 128], c=8,
                              waits=[("tS", m_moe_t)] if fcc == 0 else [],
                              sem="wo%d" % fcc)
    m_cd = S.dma_pcf("scalar", cd_sb, dg("cd_in"), c=8,
                     waits=[("tS", last_sc_mm)], sem="cds")
    m_sd = S.dma_pcf("scalar", sd_sb, dg("sd_in"), c=8, sem="cds")
    m_yev = {-4: 0, -3: 0, -2: 0, -1: 0}
    m_womm = {}
    for idx in range(32):
        fcc, ts = divmod(idx, NSEG)
        seg = idx % 4
        mlast = 0
        for dc in range(8):
            wts = []
            if dc == 0:
                wts = [("wo%d" % (fcc % 2), m_wo[fcc]), ("vS", m_attn_v),
                       ("sS", m_yev[idx - 4])]
            mlast = S.emit(
                "tensor",
                lambda e, fcc=fcc, ts=ts, dc=dc, seg=seg:
                e.matmul(P_A[:, seg * 512:(seg + 1) * 512],
                         wout_c[fcc % 2][:, dc * 128:(dc + 1) * 128],
                         outT[:, dc * 2048 + ts * 512:dc * 2048 + (ts + 1) * 512],
                         start=(dc == 0), stop=(dc == 7)),
                waits=wts, inc=("tS", 1) if dc == 7 else None)
        m_womm[fcc] = mlast
        if ts == NSEG - 1 and fcc + 2 < 8:
            m_wo[fcc + 2] = S.dma_pcf("gpsimd", wout_c[fcc % 2],
                                      dg("w_out")[:, (fcc + 2) * 128:(fcc + 3) * 128],
                                      c=8, waits=[("tS", mlast)],
                                      sem="wo%d" % (fcc % 2))
        m_yev[idx] = S.emit(
            "scalar",
            lambda e, fcc=fcc, ts=ts, seg=seg:
            e.activation(yT[:, fcc * 2048 + ts * 512:fcc * 2048 + (ts + 1) * 512],
                         P_A[:, seg * 512:(seg + 1) * 512], AF.Identity,
                         bias=bout_c[:, fcc:fcc + 1]),
            waits=[("tS", mlast)], inc=("sS", 1))
    m_yT = m_yev[31]
    if "yT" in TT:
        S.dma("sync", TT["yT"], yT[:, :], waits=[("sS", m_yT)], sem="tap")

    m_abd = 0
    m_abev = {}
    m_abdma = {}
    for tt in range(NT):
        for comp, msb, dst_row, abt in ((0, cd_sb, 0, abA_t),
                                        (1, sd_sb, 2048, abB_t)):
            for kdo in range(2):
                seg = (2 * comp + kdo) % 4
                mlast = 0
                for fcc in range(8):
                    wts = []
                    if fcc == 0:
                        wts = [("cds", m_sd), ("sS", m_yT),
                               ("vS", m_abev.get((tt - 1, comp, kdo), 0))]
                    mlast = S.emit(
                        "tensor",
                        lambda e, tt=tt, kdo=kdo, fcc=fcc, seg=seg, msb=msb:
                        e.matmul(P_A[:, seg * 512:(seg + 1) * 512],
                                 yT[:, fcc * 2048 + tt * 128:fcc * 2048 + (tt + 1) * 128],
                                 msb[:, fcc * 1024 + kdo * 512:fcc * 1024 + (kdo + 1) * 512],
                                 start=(fcc == 0), stop=(fcc == 7)),
                        waits=wts, inc=("tS", 1) if fcc == 7 else None)
                m_abev[(tt, comp, kdo)] = S.emit(
                    "vector",
                    lambda e, tt=tt, kdo=kdo, seg=seg, abt=abt:
                    e.tensor_copy(abt[tt % 2][:, kdo * 512:(kdo + 1) * 512],
                                  P_A[:, seg * 512:(seg + 1) * 512]),
                    waits=[("tS", mlast),
                           ("ab%d" % (tt % 2), m_abdma.get((tt - 2, 1), 0))],
                    inc=("vS", 1))
            m_abdma[(tt, comp)] = m_abd = S.dma(
                "sync", dg("cc2_in")[dst_row + tt * 128:dst_row + (tt + 1) * 128, :],
                abt[tt % 2][:, :], waits=[("vS", m_abev[(tt, comp, 1)])],
                sem="ab%d" % (tt % 2))
    if "ab" in TT:
        S.dma("sync", TT["ab"], dg("cc2_in")[:, :],
              waits=[("ab0", m_abdma[(14, 1)]), ("ab1", m_abdma[(15, 1)])], sem="tap")

    def cc2_fn(e):
        return e.collective_compute(
            "AllGather", ALU.bypass,
            replica_groups=[[0, 1, 2, 3], [4, 5, 6, 7]],
            ins=[dg("cc2_in")], outs=[dg("cc2_out")])
    S.emit("gpsimd", cc2_fn, waits=[("ab0", m_abdma[(14, 1)]),
                                    ("ab1", m_abdma[(15, 1)])], inc=("c2", 1))

    m_abv_last = m_abev[(15, 1, 1)]
    state["w13_extra"] = [("tS", last_sc_mm), ("vS", last_msk)]
    state["w2_extra"] = [("vS", last_msk)]
    state["silu_extra"] = [("vS", m_abv_last)]
    state["moe_prev_silu"] = [m_yev[31], m_yev[31]]
    state["moe_prev_gu"] = [m_abv_last, m_abv_last]
    m_ffn_ev[state["ev_idx"] - 3] = max(
        m_ffn_ev.get(state["ev_idx"] - 3, 0), m_abv_last)
    for n_i in range(NS):
        moe_pass(512, dg("w1_s")[n_i], dg("w3_s")[n_i], dg("w2_s")[n_i],
                 None, 1.0 / 16.0, gu_v=gus_v, w2_v=w2s_v)
    m_moe_v = m_ffn_ev[state["ev_idx"] - 1]
    m_shared_t = state["last_phB"]

    if PHASE_LIM < 6:
        S.terminal()
        return
    m_last_ab_mm = S.val("tS")
    m_gc = S.dma_pcf("scalar", gre_sb, dg("gre_in"), c=64,
                     waits=[("tS", m_last_ab_mm)], sem="cds")
    m_gc = S.dma_pcf("scalar", gim_sb, dg("gim_in"), c=64, sem="cds")
    m_gc = S.dma_pcf("scalar", gren_sb, dg("gren_in"), c=64, sem="cds")
    m_f64 = S.dma("sync", f64_sb[:, 0:16], dg("f64c_in")[:, :],
                  waits=[("tS", m_last_ab_mm)], sem="cst")
    m_f64 = S.dma("sync", f64_sb[:, 16:32], dg("f64s_in")[:, :], sem="cst")
    abg = dg("cc2_out").rearrange("(r two q a) d -> two a r q d", r=4, two=2, q=32)
    zre_v = abg[0]
    zim_v = abg[1]
    m_s1mm = {-2: 0, -1: 0}
    m_s1ev = {-2: 0, -1: 0}
    m_s1ev2 = {-2: 0, -1: 0}
    m_s1d = {}
    for a in range(64):
        wz = [("c2", 1)] if a == 0 else []
        if a >= 2:
            wz.append(("tS", m_s1mm[a - 2]))
        for r in range(4):
            S.dma("sync", zre_t[a % 2][r * 32:(r + 1) * 32, :], zre_v[a][r], waits=wz,
                  sem="zl%d" % (a % 2))
            wz = []
            m_zl = S.dma("sync", zim_t[a % 2][r * 32:(r + 1) * 32, :], zim_v[a][r],
                         sem="zl%d" % (a % 2))
        mlast = 0
        bs1 = P_A if a % 2 == 0 else P_B
        for part, po in ((0, 0), (1, 1024)):
            gA = gre_sb if part == 0 else gim_sb
            gB = gim_sb if part == 0 else gren_sb
            for kdo in range(2):
                wts = []
                if part == 0 and kdo == 0:
                    wts = [("zl%d" % (a % 2), m_zl), ("cds", m_gc),
                           ("vS", m_s1ev[a - 2]), ("sS", m_s1ev2[a - 2])]
                    if a < 2:
                        wts += [("vS", m_moe_v), ("sS", state["moe_prev_silu"][0]),
                                ("sS", state["moe_prev_silu"][1])]
                S.emit("tensor",
                       lambda e, a=a, kdo=kdo, gA=gA, po=po, bs1=bs1:
                       e.matmul(bs1[:, po + kdo * 512:po + (kdo + 1) * 512],
                                gA[:, a * 128:(a + 1) * 128],
                                zre_t[a % 2][:, kdo * 512:(kdo + 1) * 512],
                                start=True, stop=False),
                       waits=wts)
                mlast = S.emit("tensor",
                               lambda e, a=a, kdo=kdo, gB=gB, po=po, bs1=bs1:
                               e.matmul(bs1[:, po + kdo * 512:po + (kdo + 1) * 512],
                                        gB[:, a * 128:(a + 1) * 128],
                                        zim_t[a % 2][:, kdo * 512:(kdo + 1) * 512],
                                        start=False, stop=True),
                               inc=("tS", 1))
        m_s1mm[a] = mlast
        m_s1ev[a] = S.emit("vector",
                           lambda e, a=a, bs1=bs1:
                           e.tensor_copy(yev_t[a % 2][:, 0:1024], bs1[:, 0:1024]),
                           waits=[("tS", m_s1mm[a]),
                                  ("yw%d" % (a % 2), m_s1d.get(a - 2, 0))],
                           inc=("vS", 1))
        m_s1ev2[a] = S.emit("scalar",
                            lambda e, a=a, bs1=bs1:
                            e.activation(yev_t[a % 2][:, 1024:2048], bs1[:, 1024:2048], AF.Copy),
                            waits=[("tS", m_s1mm[a]),
                                   ("yw%d" % (a % 2), m_s1d.get(a - 2, 0))],
                            inc=("sS", 1))
        S.dma("scalar", dg("y_dram")[a * 128:(a + 1) * 128, :], yev_t[a % 2][:, 0:1024],
              waits=[("vS", m_s1ev[a]), ("sS", m_s1ev2[a])], sem="yw%d" % (a % 2))
        m_s1d[a] = S.dma("scalar", dg("y_dram")[8192 + a * 128:8192 + (a + 1) * 128, :],
                         yev_t[a % 2][:, 1024:2048], sem="yw%d" % (a % 2))

    if PHASE_LIM < 7:
        S.terminal()
        return
    ydv = dg("y_dram").rearrange("(two a k) d -> two a (k d)", two=2, a=64)
    m_s3mm = {-2: 0, -1: 0}
    m_s3ev = {-2: 0, -1: 0}
    m_s3d = {}
    ev3 = 0
    for kb in range(32):
        wz = []
        if kb == 0:
            wz = [("yw0", m_s1d[62]), ("yw1", m_s1d[63])]
        if kb < 2:
            wz.append(("tS", m_s1mm[63]))
        if kb >= 2:
            wz.append(("tS", m_s3mm[kb - 2]))
        S.dma("sync", s3re_t[kb % 2], ydv[0][:, kb * 4096:(kb + 1) * 4096], waits=wz,
              sem="s3l%d" % (kb % 2))
        m_s3l = S.dma("sync", s3im_t[kb % 2], ydv[1][:, kb * 4096:(kb + 1) * 4096],
                      sem="s3l%d" % (kb % 2))
        for i in range(4):
            bs3 = P_A if i % 2 == 0 else P_B
            pbase = 0
            mlast = 0
            for kdo in range(2):
                wts = []
                if i == 0 and kdo == 0:
                    wts = [("s3l%d" % (kb % 2), m_s3l), ("cst", m_f64),
                           ("vS", m_s1ev[63]), ("sS", m_s1ev2[63])]
                wts.append(("vS", m_s3ev[ev3 - 2]))
                S.emit("tensor",
                       lambda e, kb=kb, i=i, kdo=kdo, pbase=pbase, bs3=bs3:
                       e.matmul(bs3[0:16, pbase + kdo * 512:pbase + (kdo + 1) * 512],
                                f64_sb[:, 0:16],
                                s3re_t[kb % 2][:, i * 1024 + kdo * 512:i * 1024 + (kdo + 1) * 512],
                                start=True, stop=False),
                       waits=wts)
                mlast = S.emit("tensor",
                               lambda e, kb=kb, i=i, kdo=kdo, pbase=pbase, bs3=bs3:
                               e.matmul(bs3[0:16, pbase + kdo * 512:pbase + (kdo + 1) * 512],
                                        f64_sb[:, 16:32],
                                        s3im_t[kb % 2][:, i * 1024 + kdo * 512:
                                                       i * 1024 + (kdo + 1) * 512],
                                        start=False, stop=True),
                               inc=("tS", 1))
            m_s3mm[kb] = mlast
            kappa = kb * 4 + i
            m_s3ev[ev3] = S.emit(
                "vector",
                lambda e, i=i, pbase=pbase, bs3=bs3:
                e.tensor_copy(s3e_t[i % 2], bs3[0:16, pbase:pbase + 1024]),
                waits=[("tS", mlast),
                       ("xw%d" % (ev3 % 2), m_s3d.get(ev3 - 2, 0))], inc=("vS", 1))
            m_s3d[ev3] = S.dma("scalar",
                               dg("x_scr").rearrange("(l k) d -> k l d", l=16)[kappa],
                               s3e_t[i % 2],
                               waits=[("vS", m_s3ev[ev3])], sem="xw%d" % (ev3 % 2))
            ev3 += 1
    m_s3_all = m_s3d[ev3 - 1]
    if "xscr" in TT:
        S.dma("sync", TT["xscr"], dg("x_scr")[:, :],
              waits=[("xw0", m_s3d[ev3 - 2]), ("xw1", m_s3d[ev3 - 1])], sem="tap")

    if PHASE_LIM < 8:
        S.terminal()
        return
    m_fst = 0
    m_oadd = {-2: 0, -1: 0}
    m_xw_last = [m_s3d[ev3 - 2], m_s3d[ev3 - 1]]
    m_ost = {}
    for g in range(NT):
        wts = [("xw0", m_xw_last[0]), ("xw1", m_xw_last[1])] if g == 0 \
            else [("vS", m_oadd[g - 2])]
        S.dma("sync", xt_f[g % 2], x_ap[g * 128:(g + 1) * 128, :], waits=wts,
              sem="x%d" % (g % 2))
        m_fl = S.dma("sync", xf_t[g % 2], dg("x_scr")[g * 128:(g + 1) * 128, :],
                     sem="x%d" % (g % 2))
        mv = S.emit("vector",
                    lambda e, g=g:
                    e.tensor_tensor(ot_f[g % 2], xt_f[g % 2], xf_t[g % 2], op=ALU.add),
                    waits=[("x%d" % (g % 2), m_fl), ("vS", m_moe_v),
                           ("out%d" % (g % 2), m_ost.get(g - 2, 0))]
                    + ([("xw0", m_xw_last[0]), ("xw1", m_xw_last[1])] if g == 0 else []),
                    inc=("vS", 1))
        m_oadd[g] = S.emit(
            "vector",
            lambda e, g=g:
            e.tensor_tensor(ot_f[g % 2], ot_f[g % 2], ffn[:, g * 1024:(g + 1) * 1024],
                            op=ALU.add),
            waits=[("vS", mv)], inc=("vS", 1))
        m_ost[g] = m_fst = S.dma("sync", dg("y_out")[g * 128:(g + 1) * 128, :],
                                 ot_f[g % 2], waits=[("vS", m_oadd[g])],
                                 sem="out%d" % (g % 2))
    S.emit("sync", lambda e: e.nop(),
           waits=[("out0", m_ost[NT - 2]), ("out1", m_ost[NT - 1])])


_CACHE = {}


def kernel(**inputs):
    if "nc" not in _CACHE:
        _CACHE["nc"] = build_nc()
        _CACHE["const"] = _constants()
        _CACHE["masks"] = [_core_masks(s) for s in range(4)]
        _CACHE["f64"] = [_core_f64(s) for s in range(4)]
    nc = _CACHE["nc"]
    const = _CACHE["const"]

    g = {k: np.ascontiguousarray(np.asarray(v, dtype=np.float32))
         for k, v in inputs.items()}
    x = g["L0_attn_x"]
    conv_w = g["L0_attn_conv_w"]
    conv_tw = np.ascontiguousarray(
        conv_w.reshape(16, 64, 64, 4).transpose(0, 3, 2, 1))
    conv_b = g["L0_attn_conv_b"]
    shared = {
        "wqkv": g["L0_attn_Wqkv"],
        "conv_tw": conv_tw,
        "conv_b": conv_b,
        "conv_bb": np.ascontiguousarray(np.broadcast_to(conv_b, (128, D))),
        "null_k": np.ascontiguousarray(g["L0_attn_null_k"].reshape(D)),
        "null_v": np.ascontiguousarray(g["L0_attn_null_v"].reshape(D)),
        "w_out": g["L0_attn_W_out"],
        "b_out": g["L0_attn_b_out"],
        "gate_wt": np.ascontiguousarray(g["L0_ffn_gate_W"].T),
        "w1_e": g["L0_ffn_w1_e"], "w3_e": g["L0_ffn_w3_e"], "w2_e": g["L0_ffn_w2_e"],
        "w1_s": g["L0_ffn_w1_s"], "w3_s": g["L0_ffn_w3_s"], "w2_s": g["L0_ffn_w2_s"],
        "ident": const["ident"], "cd_in": const["cd_in"], "sd_in": const["sd_in"],
        "gre_in": const["gre_in"], "gim_in": const["gim_in"],
        "gren_in": const["gren_in"],
    }
    in_maps = []
    for c in range(8):
        b, s = divmod(c, 4)
        im = dict(shared)
        im["x_sl"] = np.ascontiguousarray(x[b, s * TL:(s + 1) * TL, :])
        im["masks_in"] = _CACHE["masks"][s]
        im["f64c_in"], im["f64s_in"] = _CACHE["f64"][s]
        in_maps.append(im)

    res = run_bass_kernel_spmd(nc, in_maps, core_ids=list(range(8)))
    out = np.empty((B, T, D), np.float32)
    for c in range(8):
        b, s = divmod(c, 4)
        out[b, s * TL:(s + 1) * TL, :] = res.results[c]["y_out"]
    if DEBUG_TAPS:
        _CACHE["taps"] = [{k: res.results[c]["tap_" + k] for k in DEBUG_TAPS}
                          for c in range(8)]
    return out
